# version 35
# baseline (speedup 1.0000x reference)
import numpy as np
import concourse.bass as bass
import concourse.mybir as mybir
from concourse.bass_utils import run_bass_kernel_spmd

F32 = mybir.dt.float32
BF16 = mybir.dt.bfloat16
FP8 = mybir.dt.float8e4
U8 = mybir.dt.uint8
Alu = mybir.AluOpType
Act = mybir.ActivationFunctionType
AX = mybir.AxisListType.X

T, D, CH, TC, PAIRS, NCORE = 2048, 1024, 8, 4, 8, 8
NEG = -30000.0
DBG = False
FENCE = False


def _drain(e):
    return e.drain()


def _hard(e):
    return e.drain()


def wmax(*toks):
    w = {}
    for t in toks:
        if t is None:
            continue
        if isinstance(t, dict):
            for s, v in t.items():
                if v and w.get(s, 0) < v:
                    w[s] = v
        else:
            s, v = t
            if v and w.get(s, 0) < v:
                w[s] = v
    return w


class Sched:
    def __init__(self):
        self.ops = []
        self.cnt = {}

    def op(self, eng, fn, waits=None, inc=None, amt=1):
        w = {s: v for s, v in (waits or {}).items() if v}
        self.ops.append((eng, fn, w, inc, amt))
        if inc is not None:
            self.cnt[inc] = self.cnt.get(inc, 0) + amt
            return (inc, self.cnt[inc])
        return None

    def tok(self, sem):
        return (sem, self.cnt.get(sem, 0))


def build_program():
    nc = bass.Bass("TRN2", target_bir_lowering=False)

    xT = nc.dram_tensor("xT", [D, T], F32, kind="ExternalInput").ap()
    wT = {nm: nc.dram_tensor(nm, [D, D], F32, kind="ExternalInput").ap()
          for nm in ["wqT", "wkT", "wvT", "woT", "wfT"]}
    bdr = {nm: nc.dram_tensor(nm, [128, 8], F32, kind="ExternalInput").ap()
           for nm in ["bq", "bk", "bv", "bo", "bf"]}
    ident_d = nc.dram_tensor("ident", [128, 128], F32, kind="ExternalInput").ap()
    mask_d = nc.dram_tensor("maskadd", [128, 2048], BF16, kind="ExternalInput").ap()
    outT = nc.dram_tensor("outT", [D, T], F32, kind="ExternalOutput").ap()
    qH_sc = nc.dram_tensor("qH_sc", [D, T], BF16).ap()
    qL_sc = nc.dram_tensor("qL_sc", [D, T], BF16).ap()
    kH_sc = nc.dram_tensor("kH_sc", [D, T], BF16).ap()
    kL_sc = nc.dram_tensor("kL_sc", [D, T], BF16).ap()
    v_sc = nc.dram_tensor("v_sc", [T, D], F32).ap()
    att_sc = nc.dram_tensor("att_sc", [D, T], F32).ap()
    cc_in_t = nc.dram_tensor("cc_in", [1, 64], F32)
    cc_out_t = nc.dram_tensor("cc_out", [1, 64], F32)

    AR = nc.alloc_sbuf_tensor("AR", [128, 19968], F32)
    STT = nc.alloc_sbuf_tensor("STT", [128, 5184], F32)
    masks = nc.alloc_sbuf_tensor("masks", [128, 2048], BF16)
    xnspl = nc.alloc_sbuf_tensor("xnspl", [128, 8192], BF16)
    xnspl2 = nc.alloc_sbuf_tensor("xnspl2", [128, 8192], BF16)
    wspl = nc.alloc_sbuf_tensor("wspl", [128, 16384], BF16)
    kqb = nc.alloc_sbuf_tensor("kqb", [128, 10240], BF16)
    evb = nc.alloc_sbuf_tensor("evb", [128, 2048], BF16)
    ident = nc.alloc_sbuf_tensor("ident_s", [128, 128], F32)
    ones128 = nc.alloc_sbuf_tensor("ones128", [128, 1], F32)
    ones1 = nc.alloc_sbuf_tensor("ones1", [1, 128], F32)
    bias = {nm: nc.alloc_sbuf_tensor("s" + nm, [128, 8], F32)
            for nm in ["bq", "bk", "bv", "bo", "bf"]}
    scl = nc.alloc_sbuf_tensor("scl", [128, 8], F32)
    smal = nc.alloc_sbuf_tensor("smal", [1, 128], F32)
    ccst = nc.alloc_sbuf_tensor("ccst", [1, 64], F32)
    ccrd = nc.alloc_sbuf_tensor("ccrd", [1, 64], F32)
    qt8 = nc.alloc_sbuf_tensor("qt8", [128, 1024], FP8)
    qtb = nc.alloc_sbuf_tensor("qtb", [128, 1536], BF16)
    qmsk = nc.alloc_sbuf_tensor("qmsk", [128, 512], U8)
    qt8b = nc.alloc_sbuf_tensor("qt8b", [128, 1024], FP8)
    qtbB = nc.alloc_sbuf_tensor("qtbB", [128, 512], BF16)
    qmskb = nc.alloc_sbuf_tensor("qmskb", [128, 512], U8)
    ffb = nc.alloc_sbuf_tensor("ffb", [128, 1024], BF16)

    A = lambda a, b: AR[:, a:b]
    WRING = [A(s * 1024, (s + 1) * 1024) for s in range(4)]
    XSL = [A(4096 + c * 512, 4096 + (c + 1) * 512) for c in range(CH)]
    X2B = [A(8192 + i * 512, 8192 + (i + 1) * 512) for i in range(2)]
    XNT = A(9216, 9728)
    EVQ = [A(9728 + i * 512, 9728 + (i + 1) * 512) for i in range(2)]
    VST = [A(10752 + k * 1024, 10752 + (k + 1) * 1024) for k in range(4)]
    VSTT = AR[:, 10752:14848]
    PTIL = [A(h * 2048, (h + 1) * 2048) for h in range(2)]
    PT = [A(4096 + h * 2048, 4096 + (h + 1) * 2048) for h in range(2)]
    OTS = [A(8192 + p * 512, 8192 + (p + 1) * 512) for p in range(PAIRS)]
    VBUFS = [A(12288 + i * 2048, 12288 + (i + 1) * 2048) for i in range(2)]
    RB = [A(16384 + h * 512, 16384 + (h + 1) * 512) for h in range(2)]
    OSTG = [A(17408 + i * 512, 17408 + (i + 1) * 512) for i in range(2)]
    SUMC = A(18432, 18440)
    RECC = A(18440, 18448)
    SMP = A(18448, 18576)
    EV3 = AR[0:8, 18576:18704]
    RRW = AR[0:1, 18704:19728]
    XRE = [A(4096 + c * 512, 4096 + (c + 1) * 512) for c in range(CH)]
    XN2 = [A(8192 + c * 512, 8192 + (c + 1) * 512) for c in range(CH)]
    QTMP = [A(12288 + i * 512, 12288 + (i + 1) * 512) for i in range(2)]
    FFS = [A(13312 + i * 512, 13312 + (i + 1) * 512) for i in range(4)]
    ATT = [A(15360 + i * 512, 15360 + (i + 1) * 512) for i in range(2)]

    XSH = [xnspl[:, c * 512:(c + 1) * 512] for c in range(CH)]
    XSLO = [xnspl[:, 4096 + c * 512: 4096 + (c + 1) * 512] for c in range(CH)]
    XSH2 = [xnspl2[:, c * 512:(c + 1) * 512] for c in range(CH)]
    XSLO2 = [xnspl2[:, 4096 + c * 512: 4096 + (c + 1) * 512] for c in range(CH)]
    WSH = [wspl[:, c * 1024:(c + 1) * 1024] for c in range(CH)]
    WSLO = [wspl[:, 8192 + c * 1024: 8192 + (c + 1) * 1024] for c in range(CH)]
    KH = [kqb[:, pp * 2048:(pp + 1) * 2048] for pp in range(2)]
    KL = [kqb[:, 4096 + pp * 2048: 4096 + (pp + 1) * 2048] for pp in range(2)]
    QH = [kqb[:, 8192 + pp * 512: 8192 + (pp + 1) * 512] for pp in range(2)]
    QL = [kqb[:, 9216 + pp * 512: 9216 + (pp + 1) * 512] for pp in range(2)]
    EVH = [evb[:, i * 512:(i + 1) * 512] for i in range(2)]
    EVL = [evb[:, 1024 + i * 512: 1024 + (i + 1) * 512] for i in range(2)]
    X8 = XSH
    W8 = WSH

    MROW = STT[0:1, 0:2048]
    YROW = STT[0:1, 2048:4096]
    SCR = lambda a, b: STT[0:1, 4096 + a: 4096 + b]
    AMP = STT[:, 5120:5152]
    ACOL = STT[:, 5152:5153]
    WAMP = STT[:, 5160:5168]
    WCOL = STT[:, 5168:5169]

    pb = [nc.alloc_psum_tensor("pb%d" % i, [128, 512], F32) for i in range(8)]

    sc = Sched()
    bankF = [None] * 8

    def recip_op(out_ap, in_ap, waits=None):
        sc.op("vector", _drain)
        sc.op("vector", lambda e, o=out_ap, i=in_ap: e.reciprocal(o, i), waits=waits)
        return sc.op("vector", lambda e: e.drain(), inc="ve")

    def vop_d(fn, waits=None):
        sc.op("vector", fn, waits=waits)
        return sc.op("vector", _hard, inc="ve")

    sc.op("vector", lambda e: e.memset(ones128[:], 1.0))
    sc.op("vector", lambda e: e.memset(ones1[:], 1.0))
    sc.op("vector", lambda e: e.memset(ccst[:], 0.0))
    sc.op("vector", lambda e: e.memset(smal[:], 0.0))
    t_consts = sc.op("vector", _drain, inc="ve")
    sc.op("sync", lambda e: e.dma_start(out=ident[:], in_=ident_d), inc="ld", amt=16)
    sc.op("sync", lambda e: e.dma_start(out=masks[:], in_=mask_d), inc="ld", amt=16)
    for nm in ["bq", "bk", "bv", "bo", "bf"]:
        sc.op("sync", lambda e, nm=nm: e.dma_start(
            out=bias[nm][:, :], in_=bdr[nm]), inc="ld", amt=16)
    t_const_ld = sc.tok("ld")

    wringF = [None] * 4
    g_chunk = [0]

    def ring_dma(wnm, c, extra=None):
        slot = g_chunk[0] % 4
        tk = sc.op("sync", lambda e, wnm=wnm, c=c, slot=slot: e.dma_start(
            out=WRING[slot][:], in_=wT[wnm][c * 128:(c + 1) * 128, :]),
            waits=wmax(wringF[slot], extra), inc="lw", amt=16)
        g_chunk[0] += 1
        return slot, tk

    evqF = [None, None]
    evbF = [None, None]
    wspF = [None] * CH
    xslF = [None] * CH
    xsetF = [[None] * CH, [None] * CH]
    t_xn_done = [None]
    XSHD = [XSH, XSH2]
    XSLOD = [XSLO, XSLO2]
    st = {}

    def stats_a(tc):
        c0, c1 = tc * 512, (tc + 1) * 512
        xs_tok = [None] * CH
        for c in range(CH):
            xs_tok[c] = sc.op("sync", lambda e, c=c, c0=c0, c1=c1: e.dma_start(
                out=XSL[c][:], in_=xT[c * 128:(c + 1) * 128, c0:c1]),
                waits=wmax(xslF[c]), inc="ld", amt=16)
        st[tc] = {"xs": xs_tok}

    def stats_b(tc):
        c0, c1 = tc * 512, (tc + 1) * 512
        xs_tok = st[tc]["xs"]
        sq_tok = [None] * CH
        sq_mm = [None] * CH
        for c in range(CH):
            sc.op("scalar", lambda e, c=c: e.activation(
                X2B[c % 2][:], XSL[c][:], Act.Square),
                waits=wmax(xs_tok[c], t_xn_done[0], sq_mm[c - 2] if c >= 2 else None))
            sq_tok[c] = sc.op("scalar", _drain, inc="se")
            sc.op("tensor", lambda e, c=c, st_=(c == 0), sp_=(c == CH - 1): e.matmul(
                pb[6][0:1, :], ones128[:], XSL[c][:], start=st_, stop=sp_),
                waits=wmax(xs_tok[c], t_consts, bankF[6] if c == 0 else None), inc="pe")
            sq_mm[c] = sc.op("tensor", lambda e, c=c, st_=(c == 0), sp_=(c == CH - 1): e.matmul(
                pb[7][0:1, :], ones128[:], X2B[c % 2][:], start=st_, stop=sp_),
                waits=wmax(sq_tok[c], bankF[7] if c == 0 else None), inc="pe")
        t_sums = sq_mm[CH - 1]
        mrow, yrow = MROW[0:1, c0:c1], YROW[0:1, c0:c1]
        e2, mmv = SCR(0, 512), SCR(512, 1024)
        sc.op("vector", lambda e, mrow=mrow: e.tensor_scalar(
            mrow, pb[6][0:1, :], 1.0 / 1024.0, None, op0=Alu.mult),
            waits=wmax(t_sums))
        sc.op("vector", _drain, inc="ve")
        tb = sc.op("vector", lambda e, e2=e2: e.tensor_scalar(
            e2, pb[7][0:1, :], 1.0 / 1024.0, None, op0=Alu.mult), inc="ve")
        bankF[6] = bankF[7] = tb
        sc.op("vector", lambda e, mmv=mmv, mrow=mrow: e.tensor_tensor(
            mmv, mrow, mrow, op=Alu.mult), inc="ve")
        sc.op("vector", lambda e, e2=e2, mmv=mmv: e.tensor_tensor(
            e2, e2, mmv, op=Alu.subtract), inc="ve")
        sc.op("vector", lambda e, e2=e2: e.tensor_scalar(
            e2, e2, 1e-5, None, op0=Alu.add), inc="ve")
        t_var = recip_op(mmv, e2)
        sc.op("scalar", lambda e, yrow=yrow, mmv=mmv: e.activation(
            yrow, mmv, Act.Sqrt), waits=wmax(t_var))
        st[tc]["t_y"] = sc.op("scalar", _hard, inc="se")
        st[tc]["sq_last"] = sq_tok[CH - 1]

    def stats_c(tc):
        c0, c1 = tc * 512, (tc + 1) * 512
        s = tc % 2
        mrow, yrow = MROW[0:1, c0:c1], YROW[0:1, c0:c1]
        sc.op("tensor", lambda e, mrow=mrow: e.matmul(
            pb[4][:, :], ones1[:], mrow, start=True, stop=True),
            waits=wmax(st[tc]["t_y"], t_const_ld, bankF[4]), inc="pe")
        t_yb = sc.op("tensor", lambda e, yrow=yrow: e.matmul(
            pb[5][:, :], ones1[:], yrow, start=True, stop=True),
            waits=wmax(bankF[5]), inc="pe")
        xnl_tok = [None] * CH
        prev_hi = None
        for c in range(CH):
            sc.op("vector", lambda e, c=c: e.tensor_tensor(
                X2B[c % 2][:], XSL[c][:], pb[4][:, :], op=Alu.subtract),
                waits=wmax(t_yb, st[tc]["sq_last"]), inc="ve")
            t_xn = sc.op("vector", lambda e, c=c: e.tensor_tensor(
                XNT[:], X2B[c % 2][:], pb[5][:, :], op=Alu.mult),
                waits=wmax(prev_hi, xsetF[s][c]), inc="ve")
            sc.op("vector", lambda e, c=c, tc=tc: e.tensor_reduce(
                AMP[:, tc * 8 + c: tc * 8 + c + 1], XNT[:], axis=AX,
                op=Alu.max, apply_absolute_value=True), inc="ve")
            sc.op("scalar", lambda e, c=c, s=s: e.copy(
                XSHD[s][c], XNT[:]), waits=wmax(t_xn))
            prev_hi = sc.op("scalar", _drain, inc="se")
            sc.op("vector", lambda e, c=c, s=s: e.tensor_tensor(
                XSLOD[s][c], XNT[:], XSHD[s][c], op=Alu.subtract),
                waits=wmax(prev_hi))
            xnl_tok[c] = sc.op("vector", _drain, inc="ve")
            xslF[c] = t_xn
        st[tc]["xnl"] = xnl_tok
        t_xn_done[0] = sc.tok("ve")
        bankF[4] = bankF[5] = t_xn_done[0]

    def emit_wsplit(wnm, c):
        slot, ctk = ring_dma(wnm, c)
        sc.op("scalar", lambda e, c=c, slot=slot: e.copy(
            WSH[c], WRING[slot][:]), waits=wmax(ctk, wspF[c]))
        wh = sc.op("scalar", _drain, inc="se")
        sc.op("vector", lambda e, c=c, slot=slot: e.tensor_tensor(
            WSLO[c], WRING[slot][:], WSH[c], op=Alu.subtract),
            waits=wmax(wh))
        t = sc.op("vector", _drain, inc="ve")
        wringF[slot] = t
        return t

    pend_wl = [None]

    def emit_proj(tc):
        c0, c1 = tc * 512, (tc + 1) * 512
        s = tc % 2
        xnl = st[tc]["xnl"]
        vst_last = [None]
        weights = [("wqT", "bq"), ("wkT", "bk"), ("wvT", "bv")]
        wl_tok = None
        for wi, (wnm, bnm) in enumerate(weights):
            if wi == 0:
                if tc + 1 < TC:
                    stats_a(tc + 1)
                wl_tok = pend_wl[0] if pend_wl[0] is not None else \
                    [emit_wsplit(wnm, c) for c in range(CH)]
            nxt = None
            for half in range(2):
                stopd = {}
                for c in range(CH):
                    for oc4 in range(4):
                        oc = half * 4 + oc4
                        for ps in range(3):
                            ops_ = [(WSH[c], XSHD[s][c]), (WSH[c], XSLOD[s][c]),
                                    (WSLO[c], XSHD[s][c])][ps]
                            tk = sc.op("tensor", lambda e, b=oc4, oc=oc, ops_=ops_,
                                       st_=(c == 0 and ps == 0),
                                       sp_=(c == CH - 1 and ps == 2): e.matmul(
                                pb[b][:, :], ops_[0][:, oc * 128:(oc + 1) * 128], ops_[1],
                                start=st_, stop=sp_),
                                waits=wmax(wl_tok[c], xnl[c],
                                           bankF[oc4] if (c == 0 and ps == 0) else None)
                                if ps == 0 else None, inc="pe")
                            wspF[c] = tk
                            xsetF[s][c] = tk
                        if c == CH - 1:
                            stopd[oc] = tk
                if half == 1:
                    if wi < 2:
                        nxt = [emit_wsplit(weights[wi + 1][0], c) for c in range(CH)]
                    elif tc + 1 < TC:
                        nxt = [emit_wsplit("wqT", c) for c in range(CH)]
                for oc4 in range(4):
                    oc = half * 4 + oc4
                    b = oc4
                    stop = stopd[oc]
                    if wi < 2:
                        dH, dL = (qH_sc, qL_sc) if wi == 0 else (kH_sc, kL_sc)
                        i = oc % 2
                        sc.op("vector", lambda e, b=b, oc=oc, i=i, bnm=bnm: e.tensor_scalar(
                            EVH[i], pb[b][:, :], bias[bnm][:, oc:oc + 1], None, op0=Alu.add),
                            waits=wmax(stop, evbF[i]), inc="ve")
                        sc.op("vector", lambda e, b=b, oc=oc, i=i, bnm=bnm: e.scalar_tensor_tensor(
                            EVL[i], pb[b][:, :], bias[bnm][:, oc:oc + 1], EVH[i],
                            op0=Alu.add, op1=Alu.subtract))
                        tlo = sc.op("vector", _drain, inc="ve")
                        bankF[b] = tlo
                        sc.op("scalar", lambda e, oc=oc, i=i, dH=dH, c0=c0, c1=c1: e.dma_start(
                            out=dH[oc * 128:(oc + 1) * 128, c0:c1], in_=EVH[i]),
                            waits=wmax(tlo), inc="sd", amt=16)
                        evbF[i] = sc.op("scalar", lambda e, oc=oc, i=i, dL=dL, c0=c0, c1=c1: e.dma_start(
                            out=dL[oc * 128:(oc + 1) * 128, c0:c1], in_=EVL[i]),
                            inc="sd", amt=16)
                    else:
                        ev = EVQ[oc % 2]
                        sc.op("vector", lambda e, b=b, oc=oc, ev=ev: e.tensor_scalar(
                            ev[:], pb[b][:, :], bias["bv"][:, oc:oc + 1], None, op0=Alu.add),
                            waits=wmax(stop, evqF[oc % 2]))
                        tkv = sc.op("vector", _drain, inc="ve")
                        tt = None
                        for kk in range(4):
                            tt = sc.op("tensor", lambda e, b=b, ev=ev, kk=kk: e.transpose(
                                pb[b][:, kk * 128:(kk + 1) * 128],
                                ev[:, kk * 128:(kk + 1) * 128], ident[:]),
                                waits=wmax(tkv) if kk == 0 else None, inc="pe")
                        evqF[oc % 2] = tt
                        sc.op("scalar", lambda e, b=b, oc=oc: e.copy(
                            VSTT.rearrange("p (kk j) -> p kk j", j=1024)[:, :, oc * 128:(oc + 1) * 128],
                            pb[b][:, :]), waits=wmax(tt))
                        vst_last[0] = sc.op("scalar", _drain, inc="se")
                        bankF[b] = vst_last[0]
            if wi == 0 and tc + 1 < TC:
                stats_b(tc + 1)
            if wi == 1 and tc + 1 < TC:
                stats_c(tc + 1)
            if nxt is not None and wi < 2:
                wl_tok = nxt
        pend_wl[0] = nxt if tc + 1 < TC else None
        for kk in range(4):
            sc.op("scalar", lambda e, kk=kk, c0=c0: e.dma_start(
                out=v_sc[c0 + kk * 128: c0 + (kk + 1) * 128, :], in_=VST[kk][:]),
                waits=wmax(vst_last[0]), inc="sd", amt=16)

    stats_a(0)
    stats_b(0)
    stats_c(0)
    for tc in range(TC):
        emit_proj(tc)

    t_p1_pe, t_p1_ve = sc.tok("pe"), sc.tok("ve")
    t_p1_se, t_p1_sd = sc.tok("se"), sc.tok("sd")

    sc.op("vector", lambda e: e.tensor_reduce(
        ACOL, AMP[:, :], axis=AX, op=Alu.max))
    t_r1 = sc.op("vector", _hard, inc="ve")
    t_tr = sc.op("tensor", lambda e: e.transpose(
        pb[2][0:1, 0:128], ACOL, ident[:]), waits=wmax(t_r1, bankF[2]), inc="pe")
    sc.op("vector", lambda e: e.tensor_reduce(
        ccst[0:1, 0:1], pb[2][0:1, 0:128], axis=AX, op=Alu.max),
        waits=wmax(t_tr))
    t_r2 = sc.op("vector", _hard, inc="ve")
    bankF[2] = t_r2
    t_ccw = sc.op("gpsimd", lambda e: e.dma_start(out=cc_in_t.ap(), in_=ccst[:]),
                  waits=wmax(t_r2), inc="gd", amt=16)
    t_cc = sc.op("gpsimd", lambda e: e.collective_compute(
        "AllReduce", Alu.max, replica_groups=[list(range(NCORE))],
        ins=[cc_in_t.ap().opt()], outs=[cc_out_t.ap().opt()]),
        waits=wmax(t_ccw), inc="cc")
    t_ccr = sc.op("gpsimd", lambda e: e.dma_start(out=ccrd[:], in_=cc_out_t.ap()),
                  waits=wmax(t_cc), inc="gd", amt=16)

    xspF = list(xsetF[0])
    kF = [t_p1_pe, t_p1_pe]
    qF = [t_p1_pe, t_p1_pe]
    vF = [t_p1_pe, t_p1_pe]
    ptF = [t_p1_pe, t_p1_pe]
    otsF = t_p1_pe
    smpF = [t_p1_ve]
    rbF = [t_p1_ve, t_p1_ve]
    oproj_end = t_p1_pe
    ostgF = [None, None]

    RECCF = [None]
    pend = [None]

    def emit_sums_flush():
        if pend[0] is None:
            return
        p_, t_rc_, pvev_, nkt_ = pend[0]
        pend[0] = None
        t_tr2 = None
        for j_ in range(4):
            t_tr2 = sc.op("tensor", lambda e, j_=j_: e.transpose(
                pb[4][0:1, j_ * 128:(j_ + 1) * 128], RECC[:, j_:j_ + 1], ident[:]),
                waits=wmax(t_rc_, bankF[4]) if j_ == 0 else None, inc="pe")
        for j_ in range(4):
            t_tr2 = sc.op("tensor", lambda e, j_=j_: e.transpose(
                pb[5][0:1, j_ * 128:(j_ + 1) * 128], RECC[:, 4 + j_:5 + j_], ident[:]),
                waits=wmax(bankF[5]) if j_ == 0 else None, inc="pe")
        RECCF[0] = t_tr2
        sc.op("scalar", lambda e: e.copy(RRW[0:1, 0:512], pb[4][0:1, 0:512]),
              waits=wmax(t_tr2))
        sc.op("scalar", lambda e: e.copy(RRW[0:1, 512:1024], pb[5][0:1, 0:512]))
        t_e3 = sc.op("scalar", _hard, inc="se")
        t_b0 = sc.op("tensor", lambda e: e.matmul(
            pb[4][:, :], ones1[:], RRW[0:1, 0:512], start=True, stop=True),
            waits=wmax(t_e3), inc="pe")
        t_b1 = sc.op("tensor", lambda e: e.matmul(
            pb[5][:, :], ones1[:], RRW[0:1, 512:1024], start=True, stop=True),
            inc="pe")
        t_rb0 = sc.op("scalar", lambda e: e.copy(RB[0][:], pb[4][:, :]),
                      waits=wmax(t_b0, rbF[0]), inc="se")
        sc.op("scalar", lambda e: e.copy(RB[1][:], pb[5][:, :]),
              waits=wmax(t_b1, rbF[1]))
        t_rb1 = sc.op("scalar", _drain, inc="se")
        bankF[4] = t_rb0
        bankF[5] = t_rb1
        t_n0 = sc.op("vector", lambda e, p_=p_: e.tensor_tensor(
            OTS[p_][0:64, :], OTS[p_][0:64, :], RB[0][0:64, :], op=Alu.mult),
            waits=wmax(t_rb1, pvev_), inc="ve")
        sc.op("vector", lambda e, p_=p_: e.tensor_tensor(
            OTS[p_][64:128, :], OTS[p_][64:128, :], RB[1][64:128, :], op=Alu.mult))
        t_n1 = sc.op("vector", _drain, inc="ve")
        rbF[0] = t_n0
        rbF[1] = t_n1
        sc.op("scalar", lambda e, p_=p_: e.copy(
            XSH[p_], OTS[p_][:, :]), waits=wmax(t_n1, xspF[p_]))
        oh = sc.op("scalar", _drain, inc="se")
        sc.op("vector", lambda e, p_=p_: e.tensor_tensor(
            XSLO[p_], OTS[p_][:, :], XSH[p_], op=Alu.subtract),
            waits=wmax(oh))
        ol_tok[p_] = sc.op("vector", _drain, inc="ve")

    ol_tok = [None] * CH
    wo_tok = [None] * CH
    for qt in range(TC):
        q0, q1 = qt * 512, (qt + 1) * 512
        nkt = qt + 1
        nkc = nkt * 4
        for p in range(PAIRS):
            pp = p % 2
            tq1 = sc.op("sync", lambda e, p=p, q0=q0, q1=q1, pp=pp: e.dma_start(
                out=QH[pp][:, :], in_=qH_sc[p * 128:(p + 1) * 128, q0:q1]),
                waits=wmax(t_p1_sd, qF[pp]), inc="lq", amt=16)
            tq2 = sc.op("sync", lambda e, p=p, q0=q0, q1=q1, pp=pp: e.dma_start(
                out=QL[pp][:, :], in_=qL_sc[p * 128:(p + 1) * 128, q0:q1]),
                inc="lq", amt=16)
            tk1 = sc.op("sync", lambda e, p=p, pp=pp, nkt=nkt: e.dma_start(
                out=KH[pp][:, 0:nkt * 512], in_=kH_sc[p * 128:(p + 1) * 128, 0:nkt * 512]),
                waits=wmax(kF[pp]), inc="lq", amt=16)
            tk2 = sc.op("sync", lambda e, p=p, pp=pp, nkt=nkt: e.dma_start(
                out=KL[pp][:, 0:nkt * 512], in_=kL_sc[p * 128:(p + 1) * 128, 0:nkt * 512]),
                inc="lq", amt=16)
            tqk = wmax(tq1, tq2, tk1, tk2)
            t_v = None
            for kc in range(nkc):
                t_v = sc.op("sync", lambda e, p=p, kc=kc, pp=pp: e.dma_start(
                    out=VBUFS[pp][:, kc * 128:(kc + 1) * 128],
                    in_=v_sc[kc * 128:(kc + 1) * 128, p * 128:(p + 1) * 128]),
                    waits=wmax(t_p1_sd, vF[pp]), inc="sv", amt=16)
            exp_tok = {}
            pv_ev = {}
            lastT = [None]

            def emit_S_exp(qc):
                order = [(h, kt) for kt in range(nkt) for h in range(2)]
                s_tok = {}
                for idx, (h, kt) in enumerate(order):
                    b = 2 * ((qc * nkt + kt) % 2) + h
                    t_s = None
                    for ps in range(3):
                        qop = [QH, QH, QL][ps][pp]
                        kop = [KH, KL, KH][ps][pp]
                        t_s = sc.op("tensor",
                                    lambda e, h=h, qc=qc, kt=kt, b=b, ps=ps,
                                    qop=qop, kop=kop: e.matmul(
                            pb[b][:, :],
                            qop[h * 64:(h + 1) * 64, qc * 128:(qc + 1) * 128],
                            kop[h * 64:(h + 1) * 64, kt * 512:(kt + 1) * 512],
                            start=(ps == 0), stop=(ps == 2),
                            tile_position=(h * 64, 0)),
                            waits=wmax(tqk, bankF[b], oproj_end)
                            if ps == 0 else None, inc="pe")
                    s_tok[(h, kt)] = t_s
                    if idx >= 1:
                        emit_exp(qc, *order[idx - 1], s_tok)
                for (h, kt) in order[-1:]:
                    emit_exp(qc, h, kt, s_tok)

            def emit_exp(qc, h, kt, s_tok):
                b = 2 * ((qc * nkt + kt) % 2) + h
                wexp = wmax(s_tok[(h, kt)], smpF[0], lastT[0])
                if kt == qt:
                    sc.op("vector", lambda e, b=b, qc=qc: e.tensor_tensor(
                        pb[b][:, :], pb[b][:, :],
                        masks[:, qc * 512:(qc + 1) * 512], op=Alu.add),
                        waits=wmax(s_tok[(h, kt)]))
                    t_m = sc.op("vector", _drain, inc="ve")
                    wexp = wmax(t_m, smpF[0], lastT[0])
                col = h * 64 + qc * 16 + kt
                sc.op("scalar", lambda e, h=h, kt=kt, b=b, col=col: e.activation(
                    PTIL[h][:, kt * 512:(kt + 1) * 512], pb[b][:, :], Act.Exp,
                    bias=0.0, scale=1.0 / 32.0,
                    accum_out=SMP[:, col:col + 1]), waits=wexp)
                te = sc.op("scalar", _drain, inc="se")
                bankF[b] = te
                exp_tok[(qc, h, kt)] = te

            def emit_tpv(qc):
                tpF = {}
                tt = None
                for h in range(2):
                    for g in range(nkt):
                        tbk = 4 + ((h * nkt + g) % 2)
                        for i in range(4):
                            kc = g * 4 + i
                            tt = sc.op("tensor", lambda e, h=h, kc=kc, i=i, tbk=tbk: e.transpose(
                                pb[tbk][:, i * 128:(i + 1) * 128],
                                PTIL[h][:, kc * 128:(kc + 1) * 128], ident[:]),
                                waits=wmax(exp_tok[(qc, h, g)], bankF[tbk],
                                           ptF[h] if g == 0 else None)
                                if i == 0 else None, inc="pe")
                        eng, sem = "vector", "ve"
                        sc.op(eng, lambda e, h=h, g=g, tbk=tbk: e.tensor_copy(
                            PT[h][:, g * 512:(g + 1) * 512], pb[tbk][:, :]),
                            waits=wmax(tt))
                        tev = sc.op(eng, _drain, inc=sem)
                        bankF[tbk] = tev
                        tpF[h] = tev
                lastT[0] = tt
                tpv = {}
                for kc in range(nkc):
                    tpv[0] = sc.op("tensor", lambda e, kc=kc, nkc=nkc, pp=pp: e.matmul(
                        pb[6][0:64, 0:128], VBUFS[pp][:, kc * 128: kc * 128 + 64],
                        PT[0][:, kc * 128:(kc + 1) * 128],
                        start=(kc == 0), stop=(kc == nkc - 1), tile_position=(0, 0)),
                        waits=wmax(t_v, tpF[0], bankF[6]) if kc == 0 else None,
                        inc="pe")
                    tpv[1] = sc.op("tensor", lambda e, kc=kc, nkc=nkc, pp=pp: e.matmul(
                        pb[7][64:128, 0:128], VBUFS[pp][:, kc * 128 + 64: kc * 128 + 128],
                        PT[1][:, kc * 128:(kc + 1) * 128],
                        start=(kc == 0), stop=(kc == nkc - 1), tile_position=(0, 64)),
                        waits=wmax(t_v, tpF[1], bankF[7]) if kc == 0 else None,
                        inc="pe")
                t_e0 = sc.op("vector", lambda e, p=p, qc=qc: e.tensor_copy(
                    OTS[p][0:64, qc * 128:(qc + 1) * 128], pb[6][0:64, 0:128]),
                    waits=wmax(tpv[0], otsF if qc == 0 else None,
                               t_p1_sd if qc == 0 else None), inc="ve")
                sc.op("vector", lambda e, p=p, qc=qc: e.tensor_copy(
                    OTS[p][64:128, qc * 128:(qc + 1) * 128], pb[7][64:128, 0:128]),
                    waits=wmax(tpv[1]))
                t_e1 = sc.op("vector", _drain, inc="ve")
                bankF[6] = t_e0
                bankF[7] = t_e1
                pv_ev[qc] = t_e1

            emit_S_exp(0)
            emit_sums_flush()
            for qc in range(1, 4):
                emit_tpv(qc - 1)
                emit_S_exp(qc)
            t_slast = sc.tok("pe")
            emit_tpv(3)
            t_rd1 = None
            for h in range(2):
                for qc in range(4):
                    cb = h * 64 + qc * 16
                    t_rd1 = sc.op("vector", lambda e, h=h, qc=qc, cb=cb, nkt=nkt: e.tensor_reduce(
                        SUMC[:, h * 4 + qc: h * 4 + qc + 1], SMP[:, cb:cb + nkt],
                        axis=AX, op=Alu.add),
                        waits=wmax(exp_tok[(qc, h, qt)]), inc="ve")
            smpF[0] = t_rd1
            t_rc = recip_op(RECC[:, 0:8], SUMC[:, 0:8], waits=wmax(t_rd1, RECCF[0]))
            pend[0] = (p, t_rc, pv_ev[3], nkt)
            end_pe = sc.tok("pe")
            kF[pp] = t_slast
            qF[pp] = t_slast
            vF[pp] = end_pe
            ptF = [sc.tok("pe"), sc.tok("pe")]
        t_pairs_pe = sc.tok("pe")
        if qt == 0:
            for s in range(4):
                wringF[s] = wmax(wringF[s], t_pairs_pe)
            for c in range(CH):
                slot, ctk = ring_dma("woT", c)
                sc.op("scalar", lambda e, c=c, slot=slot: e.copy(
                    WSH[c], WRING[slot][:]), waits=wmax(ctk, wspF[c]))
                wh = sc.op("scalar", _drain, inc="se")
                sc.op("vector", lambda e, c=c, slot=slot: e.tensor_tensor(
                    WSLO[c], WRING[slot][:], WSH[c], op=Alu.subtract),
                    waits=wmax(wh))
                wo_tok[c] = sc.op("vector", _drain, inc="ve")
                wringF[slot] = wo_tok[c]
        wl_tok = wo_tok
        stop_tok = [None] * CH

        def oproj_mm(c, oc):
            tkk = None
            for ps in range(3):
                ops_ = [(WSH[c], XSH[c]), (WSH[c], XSLO[c]), (WSLO[c], XSH[c])][ps]
                tkk = sc.op("tensor", lambda e, c=c, oc=oc, ops_=ops_,
                            st_=(c == 0 and ps == 0),
                            sp_=(c == CH - 1 and ps == 2): e.matmul(
                    pb[oc][:, :], ops_[0][:, oc * 128:(oc + 1) * 128], ops_[1],
                    start=st_, stop=sp_),
                    waits=wmax(wl_tok[c], ol_tok[c],
                               bankF[oc] if (c == 0 and ps == 0) else None)
                    if ps == 0 else None, inc="pe")
            if c == CH - 1:
                stop_tok[oc] = tkk
            wspF[c] = tkk
            xspF[c] = tkk

        def oproj_evac(oc):
            i = oc % 2
            sc.op("vector", lambda e, oc=oc, i=i: e.tensor_scalar(
                OSTG[i][:], pb[oc][:, :], bias["bo"][:, oc:oc + 1], None, op0=Alu.add),
                waits=wmax(stop_tok[oc], ostgF[i]))
            tev = sc.op("vector", _drain, inc="ve")
            bankF[oc] = tev
            ostgF[i] = sc.op("scalar", lambda e, oc=oc, i=i, q0=q0, q1=q1: e.dma_start(
                out=att_sc[oc * 128:(oc + 1) * 128, q0:q1], in_=OSTG[i][:]),
                waits=wmax(tev), inc="sg", amt=16)

        for c in range(CH - 1):
            for oc in range(4):
                oproj_mm(c, oc)
        emit_sums_flush()
        for oc in range(4):
            oproj_mm(CH - 1, oc)
        for oc in range(4):
            oproj_evac(oc)
        for c in range(CH):
            for oc in range(4, CH):
                oproj_mm(c, oc)
        for oc in range(4, CH):
            oproj_evac(oc)
        oproj_end = sc.tok("pe")
        otsF = oproj_end

    t_p3_pe, t_p3_ve, t_p3_se = sc.tok("pe"), sc.tok("ve"), sc.tok("se")
    t_p3_sd = sc.tok("sd")
    t_p3_sg = sc.tok("sg")

    t_wred = None
    for c in range(CH):
        slot, ctk = ring_dma("wfT", c, extra=wmax(t_p3_pe, t_p3_se))
        sc.op("vector", lambda e, c=c, slot=slot: e.tensor_reduce(
            WAMP[:, c:c + 1], WRING[slot][:], axis=AX, op=Alu.max,
            apply_absolute_value=True), waits=wmax(ctk, t_p3_ve))
        t_wred = sc.op("vector", _hard, inc="ve")
        wringF[slot] = t_wred
    t_wnop = sc.op("tensor", lambda e: e.nop(), waits=wmax(t_wred, t_p3_pe), inc="pe")
    sc.op("vector", lambda e: e.tensor_reduce(
        WCOL, WAMP[:, 0:8], axis=AX, op=Alu.max))
    t_wc = sc.op("vector", _hard, inc="ve")
    t_wtr = sc.op("tensor", lambda e: e.transpose(
        pb[2][0:1, 0:128], WCOL, ident[:]), waits=wmax(t_wc, bankF[2], t_wnop), inc="pe")
    sc.op("vector", lambda e: e.tensor_reduce(
        smal[0:1, 2:3], pb[2][0:1, 0:128], axis=AX, op=Alu.max),
        waits=wmax(t_wtr))
    t_wamax = sc.op("vector", _hard, inc="ve")
    bankF[2] = t_wamax
    vop_d(lambda e: e.tensor_scalar(
        smal[0:1, 4:5], ccrd[0:1, 0:1], 1e-12, None, op0=Alu.max),
        waits=wmax(t_ccr, t_wamax))
    recip_op(smal[0:1, 6:7], smal[0:1, 4:5])
    vop_d(lambda e: e.tensor_scalar(
        smal[0:1, 6:7], smal[0:1, 6:7], 448.0, 448.0, op0=Alu.mult, op1=Alu.min))
    recip_op(smal[0:1, 7:8], smal[0:1, 6:7])
    vop_d(lambda e: e.tensor_scalar(
        smal[0:1, 9:10], smal[0:1, 2:3], 1e-12, None, op0=Alu.max))
    recip_op(smal[0:1, 10:11], smal[0:1, 9:10])
    vop_d(lambda e: e.tensor_scalar(
        smal[0:1, 10:11], smal[0:1, 10:11], 448.0, 448.0, op0=Alu.mult, op1=Alu.min))
    recip_op(smal[0:1, 11:12], smal[0:1, 10:11])
    t_cmul = vop_d(lambda e: e.tensor_tensor(
        smal[0:1, 12:13], smal[0:1, 7:8], smal[0:1, 11:12], op=Alu.mult))
    bc_done = None
    prev_bc = None
    for src, dst in [(12, 0), (6, 1), (10, 2)]:
        tbc = sc.op("tensor", lambda e, src=src: e.matmul(
            pb[3][:, 0:1], ones1[:], smal[0:1, src:src + 1], start=True, stop=True),
            waits=wmax(t_cmul, bankF[3], prev_bc), inc="pe")
        sc.op("vector", lambda e, dst=dst: e.tensor_copy(
            scl[:, dst:dst + 1], pb[3][:, 0:1]), waits=wmax(tbc))
        bc_done = sc.op("vector", _hard, inc="ve")
        prev_bc = bc_done
        bankF[3] = bc_done

    QSET = [
        {"t0": QTMP[0], "t1": QTMP[1], "q8": qt8,
         "ob": qtb[:, 0:512], "mk": qmsk[:, 0:512]},
        {"t0": STT[:, 4096:4608], "t1": STT[:, 4608:5120], "q8": qt8b,
         "ob": qtbB[:, 0:512], "mk": qmskb[:, 0:512]},
    ]
    lastq = [{"se": None, "ve": None}, {"se": None, "ve": None}]

    def quant_q(src_ap, dst_ap, scale_col, qs, extra=None):
        B, L = QSET[qs], lastq[qs]
        t_in = sc.op("vector", lambda e, B=B, src_ap=src_ap, scale_col=scale_col:
                     e.tensor_scalar(B["t0"][:], src_ap, scl[:, scale_col:scale_col + 1],
                                     448.0, op0=Alu.mult, op1=Alu.min),
                     waits=wmax(bc_done, L["ve"], L["se"], extra), inc="ve")
        sc.op("vector", lambda e, B=B: e.tensor_scalar(
            B["t0"][:], B["t0"][:], -448.0, None, op0=Alu.max), inc="ve")
        sc.op("vector", lambda e, B=B: e.tensor_copy(B["q8"][:, 0:512], B["t0"][:]))
        t_8a = sc.op("vector", _drain, inc="ve")
        sc.op("vector", lambda e, B=B: e.tensor_scalar(
            B["t1"][:], B["t0"][:], 0.5, None, op0=Alu.mult), inc="ve")
        sc.op("vector", lambda e, B=B: e.tensor_copy(B["q8"][:, 512:1024], B["t1"][:]))
        t_8b = sc.op("vector", _drain, inc="ve")
        sc.op("scalar", lambda e, B=B, dst_ap=dst_ap: e.copy(dst_ap, B["q8"][:, 0:512]),
              waits=wmax(t_8a, L["se"]))
        t_da = sc.op("scalar", _drain, inc="se")
        sc.op("scalar", lambda e, B=B: e.mul(B["ob"], B["q8"][:, 512:1024], 2.0),
              waits=wmax(t_8b))
        sc.op("scalar", lambda e, B=B: e.activation(B["t1"][:], B["t0"][:], Act.Abs))
        t_ob = sc.op("scalar", _drain, inc="se")
        sc.op("vector", lambda e, B=B: e.tensor_scalar(
            B["mk"], B["t1"][:], 240.0, None, op0=Alu.is_ge),
            waits=wmax(t_ob), inc="ve")
        sc.op("vector", lambda e, B=B, dst_ap=dst_ap: e.copy_predicated(
            dst_ap, B["mk"], B["ob"]), waits=wmax(t_da))
        t_cp = sc.op("vector", _drain, inc="ve")
        L["se"] = t_ob
        L["ve"] = t_cp
        return t_cp, t_in

    w8_tok = None
    for c in range(CH):
        slot, ctk = ring_dma("wfT", c)
        rdF = None
        for half in range(2):
            w8_tok, t_rd = quant_q(WRING[slot][:, half * 512:(half + 1) * 512],
                                   W8[c][:, half * 512:(half + 1) * 512], 2, half,
                                   extra=wmax(ctk, wspF[c] if half == 0 else None))
            rdF = wmax(rdF, t_rd)
        wringF[slot] = rdF
    t_w8nop = sc.op("tensor", lambda e: e.nop(),
                    waits=wmax(w8_tok, lastq[0]["ve"], lastq[1]["ve"]), inc="pe")

    xreF = [None] * CH
    xn2F = [None] * CH
    x8F = [None] * CH
    attF = [None, None]
    ffbF = [None, None]
    ffsF = [None] * 4
    for tc in range(TC):
        c0, c1 = tc * 512, (tc + 1) * 512
        xr_tok = [None] * CH
        for c in range(CH):
            xr_tok[c] = sc.op("sync", lambda e, c=c, c0=c0, c1=c1: e.dma_start(
                out=XRE[c][:], in_=xT[c * 128:(c + 1) * 128, c0:c1]),
                waits=wmax(xreF[c], t_p3_pe if tc == 0 else None), inc="ld", amt=16)
        mrow, yrow = MROW[0:1, c0:c1], YROW[0:1, c0:c1]
        sc.op("tensor", lambda e, mrow=mrow: e.matmul(
            pb[4][:, :], ones1[:], mrow, start=True, stop=True),
            waits=wmax(bankF[4], t_w8nop), inc="pe")
        t_yb = sc.op("tensor", lambda e, yrow=yrow: e.matmul(
            pb[5][:, :], ones1[:], yrow, start=True, stop=True),
            waits=wmax(bankF[5]), inc="pe")
        xn_t = [None] * CH
        for c in range(CH):
            sc.op("vector", lambda e, c=c: e.tensor_tensor(
                XN2[c][:], XRE[c][:], pb[4][:, :], op=Alu.subtract),
                waits=wmax(t_yb, xr_tok[c], xn2F[c], lastq[0]["ve"], lastq[1]["ve"],
                           t_p3_ve if tc == 0 and c == 0 else None), inc="ve")
            xn_t[c] = sc.op("vector", lambda e, c=c: e.tensor_tensor(
                XN2[c][:], XN2[c][:], pb[5][:, :], op=Alu.mult), inc="ve")
        bankF[4] = sc.tok("ve")
        bankF[5] = sc.tok("ve")
        x8_t = [None] * CH
        for c in range(CH):
            x8_t[c], _ = quant_q(XN2[c][:], X8[c][:], 1, c % 2,
                                 extra=wmax(xn_t[c], x8F[c],
                                            xspF[c] if tc == 0 else None))
        stop_tok = [None] * CH
        for c in range(CH):
            for oc in range(CH):
                tkk = sc.op("tensor", lambda e, c=c, oc=oc: e.matmul(
                    pb[oc][:, :], W8[c][:, oc * 128:(oc + 1) * 128], X8[c][:],
                    start=(c == 0), stop=(c == CH - 1)),
                    waits=wmax(x8_t[c], bankF[oc] if c == 0 else None,
                               t_p3_pe if tc == 0 and c == 0 else None), inc="pe")
                if c == CH - 1:
                    stop_tok[oc] = tkk
        for c in range(CH):
            x8F[c] = stop_tok[CH - 1]
            xn2F[c] = xn_t[c]
        for oc in range(CH):
            fb = oc % 2
            ta = sc.op("scalar", lambda e, oc=oc, fb=fb, c0=c0, c1=c1: e.dma_start(
                out=ATT[fb][:], in_=att_sc[oc * 128:(oc + 1) * 128, c0:c1]),
                waits=wmax(attF[fb], t_p3_sg), inc="sg", amt=16)
            t_fe = sc.op("vector", lambda e, oc=oc, fb=fb: e.tensor_scalar(
                ffb[:, fb * 512:(fb + 1) * 512], pb[oc][:, :],
                scl[:, 0:1], bias["bf"][:, oc:oc + 1], op0=Alu.mult, op1=Alu.add),
                waits=wmax(stop_tok[oc], ffbF[fb]), inc="ve")
            bankF[oc] = t_fe
            s0, s1 = fb * 2, fb * 2 + 1
            t_a1 = sc.op("vector", lambda e, oc=oc, fb=fb, s0=s0: e.tensor_tensor(
                FFS[s0][:], XRE[oc][:], ATT[fb][:], op=Alu.add),
                waits=wmax(xr_tok[oc], ta, ffsF[s0]), inc="ve")
            attF[fb] = t_a1
            sc.op("vector", lambda e, s0=s0, s1=s1, fb=fb: e.tensor_tensor(
                FFS[s1][:], FFS[s0][:], ffb[:, fb * 512:(fb + 1) * 512],
                op=Alu.add), waits=wmax(t_fe, ffsF[s1]))
            t_a2 = sc.op("vector", _drain, inc="ve")
            ffbF[fb] = t_a2
            ffsF[s0] = t_a2
            xreF[oc] = t_a2
            t_out = sc.op("scalar", lambda e, oc=oc, c0=c0, c1=c1, s1=s1: e.dma_start(
                out=outT[oc * 128:(oc + 1) * 128, c0:c1], in_=FFS[s1][:]),
                waits=wmax(t_a2), inc="so", amt=16)
            ffsF[s1] = t_out

    if DBG:
        dqh = nc.dram_tensor("dqh", [D, T], BF16, kind="ExternalOutput").ap()
        dql = nc.dram_tensor("dql", [D, T], BF16, kind="ExternalOutput").ap()
        dkh = nc.dram_tensor("dkh", [D, T], BF16, kind="ExternalOutput").ap()
        dkl = nc.dram_tensor("dkl", [D, T], BF16, kind="ExternalOutput").ap()
        dv = nc.dram_tensor("dv", [T, D], F32, kind="ExternalOutput").ap()
        datt = nc.dram_tensor("datt", [D, T], F32, kind="ExternalOutput").ap()
        dsml = nc.dram_tensor("dsml", [1, 128], F32, kind="ExternalOutput").ap()
        dstt = nc.dram_tensor("dstt", [128, 5184], F32, kind="ExternalOutput").ap()
        t_all = wmax(sc.tok("ve"), sc.tok("se"), sc.tok("pe"), sc.tok("sd"), sc.tok("so"))
        sc.op("gpsimd", lambda e: e.dma_start(out=dqh, in_=qH_sc), waits=t_all, inc="so", amt=16)
        sc.op("gpsimd", lambda e: e.dma_start(out=dql, in_=qL_sc), inc="so", amt=16)
        sc.op("gpsimd", lambda e: e.dma_start(out=dkh, in_=kH_sc), inc="so", amt=16)
        sc.op("gpsimd", lambda e: e.dma_start(out=dkl, in_=kL_sc), inc="so", amt=16)
        sc.op("gpsimd", lambda e: e.dma_start(out=dv, in_=v_sc), inc="so", amt=16)
        sc.op("gpsimd", lambda e: e.dma_start(out=datt, in_=att_sc), inc="so", amt=16)
        sc.op("gpsimd", lambda e: e.dma_start(out=dsml, in_=smal[:]), inc="so", amt=16)
        sc.op("gpsimd", lambda e: e.dma_start(out=dstt, in_=STT[:, :]), inc="so", amt=16)
        dx8 = nc.dram_tensor("dx8", [128, 8192], BF16, kind="ExternalOutput").ap()
        dw8 = nc.dram_tensor("dw8", [128, 16384], BF16, kind="ExternalOutput").ap()
        sc.op("gpsimd", lambda e: e.dma_start(out=dx8, in_=xnspl[:, :]), inc="so", amt=16)
        sc.op("gpsimd", lambda e: e.dma_start(out=dw8, in_=wspl[:, :]), inc="so", amt=16)
    sc.op("gpsimd", lambda e: e.nop(), waits=wmax(sc.tok("so")))

    sems = {s: nc.alloc_semaphore(s) for s in
            ["ld", "sd", "gd", "pe", "ve", "se", "cc", "so",
             "lw", "lq", "lr", "sv", "sg"]}
    DMA_SEMS = {"ld", "sd", "gd", "so", "lw", "lq", "lr", "sv", "sg"}
    sc_counts = {}
    with nc.Block() as block:
        def mk(engname):
            def body(eng):
                seen = {}
                prev_dma = {}
                last_ins = [None]
                for (e, fn, waits, inc, amt) in sc.ops:
                    if e != engname:
                        continue
                    if fn is _drain and not FENCE and inc is not None:
                        if last_ins[0] is not None:
                            last_ins[0].then_inc(sems[inc], amt)
                            continue
                    w = dict(waits)
                    if inc in DMA_SEMS:
                        pv = prev_dma.get(inc, 0)
                        if pv and w.get(inc, 0) < pv:
                            w[inc] = pv
                    for s, v in w.items():
                        if seen.get(s, 0) < v:
                            eng.wait_ge(sems[s], v)
                            seen[s] = v
                    ins = fn(eng)
                    if ins is not None:
                        last_ins[0] = ins
                    if inc is not None and ins is not None:
                        ins.then_inc(sems[inc], amt)
                        if inc in DMA_SEMS:
                            prev_dma[inc] = prev_dma.get(inc, 0) + amt
            return body
        block.sync(mk("sync"))
        block.scalar(mk("scalar"))
        block.vector(mk("vector"))
        block.tensor(mk("tensor"))
        block.gpsimd(mk("gpsimd"))
    return nc


_CACHE = {}


def _get_nc():
    if "nc" not in _CACHE:
        _CACHE["nc"] = build_program()
    return _CACHE["nc"]


def kernel(L0_attn_x, L0_attn_Wq_w, L0_attn_Wq_b, L0_attn_Wk_w, L0_attn_Wk_b,
           L0_attn_Wv_w, L0_attn_Wv_b, L0_attn_Wo_w, L0_attn_Wo_b,
           L0_ffn_weight, L0_ffn_bias):
    import ml_dtypes
    x = np.ascontiguousarray(np.asarray(L0_attn_x, dtype=np.float32))
    ws = {
        "wqT": np.ascontiguousarray(np.asarray(L0_attn_Wq_w, dtype=np.float32).T),
        "wkT": np.ascontiguousarray(np.asarray(L0_attn_Wk_w, dtype=np.float32).T),
        "wvT": np.ascontiguousarray(np.asarray(L0_attn_Wv_w, dtype=np.float32).T),
        "woT": np.ascontiguousarray(np.asarray(L0_attn_Wo_w, dtype=np.float32).T),
        "wfT": np.ascontiguousarray(np.asarray(L0_ffn_weight, dtype=np.float32).T),
    }

    def _btile(b):
        return np.ascontiguousarray(
            np.asarray(b, dtype=np.float32).reshape(8, 128).T)
    bs = {
        "bq": _btile(L0_attn_Wq_b), "bk": _btile(L0_attn_Wk_b),
        "bv": _btile(L0_attn_Wv_b), "bo": _btile(L0_attn_Wo_b),
        "bf": _btile(L0_ffn_bias),
    }
    ident = np.eye(128, dtype=np.float32)
    i_ = np.arange(128)[:, None]
    j_ = np.arange(512)[None, :]
    maskadd = np.concatenate(
        [np.where(o * 128 + i_ >= j_, 0.0, NEG).astype(np.float32)
         for o in range(4)], axis=1).astype(ml_dtypes.bfloat16)

    nc = _get_nc()
    in_maps = []
    for b in range(NCORE):
        im = {"xT": np.ascontiguousarray(x[b].T), "ident": ident, "maskadd": maskadd}
        im.update(ws)
        im.update(bs)
        in_maps.append(im)
    res = run_bass_kernel_spmd(nc, in_maps, core_ids=list(range(NCORE)))
    out = np.stack([np.asarray(res.results[b]["outT"]).T for b in range(NCORE)])
    return np.ascontiguousarray(out).astype(np.float32, copy=False)


# version 36
# speedup vs baseline: 1.0030x; 1.0030x over previous
import numpy as np
import concourse.bass as bass
import concourse.mybir as mybir
from concourse.bass_utils import run_bass_kernel_spmd

F32 = mybir.dt.float32
BF16 = mybir.dt.bfloat16
FP8 = mybir.dt.float8e4
U8 = mybir.dt.uint8
Alu = mybir.AluOpType
Act = mybir.ActivationFunctionType
AX = mybir.AxisListType.X

T, D, CH, TC, PAIRS, NCORE = 2048, 1024, 8, 4, 8, 8
NEG = -30000.0
DBG = False
FENCE = False


def _drain(e):
    return e.drain()


def _hard(e):
    return e.drain()


def wmax(*toks):
    w = {}
    for t in toks:
        if t is None:
            continue
        if isinstance(t, dict):
            for s, v in t.items():
                if v and w.get(s, 0) < v:
                    w[s] = v
        else:
            s, v = t
            if v and w.get(s, 0) < v:
                w[s] = v
    return w


class Sched:
    def __init__(self):
        self.ops = []
        self.cnt = {}

    def op(self, eng, fn, waits=None, inc=None, amt=1):
        w = {s: v for s, v in (waits or {}).items() if v}
        self.ops.append((eng, fn, w, inc, amt))
        if inc is not None:
            self.cnt[inc] = self.cnt.get(inc, 0) + amt
            return (inc, self.cnt[inc])
        return None

    def tok(self, sem):
        return (sem, self.cnt.get(sem, 0))


def build_program():
    nc = bass.Bass("TRN2", target_bir_lowering=False)

    xT = nc.dram_tensor("xT", [D, T], F32, kind="ExternalInput").ap()
    wT = {nm: nc.dram_tensor(nm, [D, D], F32, kind="ExternalInput").ap()
          for nm in ["wqT", "wkT", "wvT", "woT", "wfT"]}
    bdr = {nm: nc.dram_tensor(nm, [128, 8], F32, kind="ExternalInput").ap()
           for nm in ["bq", "bk", "bv", "bo", "bf"]}
    ident_d = nc.dram_tensor("ident", [128, 128], F32, kind="ExternalInput").ap()
    mask_d = nc.dram_tensor("maskadd", [128, 2048], BF16, kind="ExternalInput").ap()
    outT = nc.dram_tensor("outT", [D, T], F32, kind="ExternalOutput").ap()
    qH_sc = nc.dram_tensor("qH_sc", [D, T], BF16).ap()
    qL_sc = nc.dram_tensor("qL_sc", [D, T], BF16).ap()
    kH_sc = nc.dram_tensor("kH_sc", [D, T], BF16).ap()
    kL_sc = nc.dram_tensor("kL_sc", [D, T], BF16).ap()
    v_sc = nc.dram_tensor("v_sc", [T, D], F32).ap()
    att_sc = nc.dram_tensor("att_sc", [D, T], F32).ap()
    cc_in_t = nc.dram_tensor("cc_in", [1, 64], F32)
    cc_out_t = nc.dram_tensor("cc_out", [1, 64], F32)

    AR = nc.alloc_sbuf_tensor("AR", [128, 19968], F32)
    STT = nc.alloc_sbuf_tensor("STT", [128, 5184], F32)
    masks = nc.alloc_sbuf_tensor("masks", [128, 2048], BF16)
    xnspl = nc.alloc_sbuf_tensor("xnspl", [128, 8192], BF16)
    xnspl2 = nc.alloc_sbuf_tensor("xnspl2", [128, 8192], BF16)
    wspl = nc.alloc_sbuf_tensor("wspl", [128, 16384], BF16)
    kqb = nc.alloc_sbuf_tensor("kqb", [128, 10240], BF16)
    evb = nc.alloc_sbuf_tensor("evb", [128, 2048], BF16)
    ident = nc.alloc_sbuf_tensor("ident_s", [128, 128], F32)
    ones128 = nc.alloc_sbuf_tensor("ones128", [128, 1], F32)
    ones1 = nc.alloc_sbuf_tensor("ones1", [1, 128], F32)
    bias = {nm: nc.alloc_sbuf_tensor("s" + nm, [128, 8], F32)
            for nm in ["bq", "bk", "bv", "bo", "bf"]}
    scl = nc.alloc_sbuf_tensor("scl", [128, 8], F32)
    smal = nc.alloc_sbuf_tensor("smal", [1, 128], F32)
    ccst = nc.alloc_sbuf_tensor("ccst", [1, 64], F32)
    ccrd = nc.alloc_sbuf_tensor("ccrd", [1, 64], F32)
    qt8 = nc.alloc_sbuf_tensor("qt8", [128, 1024], FP8)
    qtb = nc.alloc_sbuf_tensor("qtb", [128, 1536], BF16)
    qmsk = nc.alloc_sbuf_tensor("qmsk", [128, 512], U8)
    qt8b = nc.alloc_sbuf_tensor("qt8b", [128, 1024], FP8)
    qtbB = nc.alloc_sbuf_tensor("qtbB", [128, 512], BF16)
    qmskb = nc.alloc_sbuf_tensor("qmskb", [128, 512], U8)
    ffb = nc.alloc_sbuf_tensor("ffb", [128, 1024], BF16)

    A = lambda a, b: AR[:, a:b]
    WRING = [A(s * 1024, (s + 1) * 1024) for s in range(4)]
    XSL = [A(4096 + c * 512, 4096 + (c + 1) * 512) for c in range(CH)]
    X2B = [A(8192 + i * 512, 8192 + (i + 1) * 512) for i in range(2)]
    XNT = A(9216, 9728)
    EVQ = [A(9728 + i * 512, 9728 + (i + 1) * 512) for i in range(2)]
    VST = [A(10752 + k * 1024, 10752 + (k + 1) * 1024) for k in range(4)]
    VSTT = AR[:, 10752:14848]
    PTIL = [A(h * 2048, (h + 1) * 2048) for h in range(2)]
    PT = [A(4096 + h * 2048, 4096 + (h + 1) * 2048) for h in range(2)]
    OTS = [A(8192 + p * 512, 8192 + (p + 1) * 512) for p in range(PAIRS)]
    VBUFS = [A(12288 + i * 2048, 12288 + (i + 1) * 2048) for i in range(2)]
    RB = [A(16384 + h * 512, 16384 + (h + 1) * 512) for h in range(2)]
    OSTG = [A(17408 + i * 512, 17408 + (i + 1) * 512) for i in range(2)]
    SUMC = A(18432, 18440)
    RECC = A(18440, 18448)
    SMP = A(18448, 18576)
    EV3 = AR[0:8, 18576:18704]
    RRW = AR[0:1, 18704:19728]
    XRE = [A(4096 + c * 512, 4096 + (c + 1) * 512) for c in range(CH)]
    XN2 = [A(8192 + c * 512, 8192 + (c + 1) * 512) for c in range(CH)]
    QTMP = [A(12288 + i * 512, 12288 + (i + 1) * 512) for i in range(2)]
    FFS = [A(13312 + i * 512, 13312 + (i + 1) * 512) for i in range(4)]
    ATT = [A(15360 + i * 512, 15360 + (i + 1) * 512) for i in range(2)]

    XSH = [xnspl[:, c * 512:(c + 1) * 512] for c in range(CH)]
    XSLO = [xnspl[:, 4096 + c * 512: 4096 + (c + 1) * 512] for c in range(CH)]
    XSH2 = [xnspl2[:, c * 512:(c + 1) * 512] for c in range(CH)]
    XSLO2 = [xnspl2[:, 4096 + c * 512: 4096 + (c + 1) * 512] for c in range(CH)]
    WSH = [wspl[:, c * 1024:(c + 1) * 1024] for c in range(CH)]
    WSLO = [wspl[:, 8192 + c * 1024: 8192 + (c + 1) * 1024] for c in range(CH)]
    KH = [kqb[:, pp * 2048:(pp + 1) * 2048] for pp in range(2)]
    KL = [kqb[:, 4096 + pp * 2048: 4096 + (pp + 1) * 2048] for pp in range(2)]
    QH = [kqb[:, 8192 + pp * 512: 8192 + (pp + 1) * 512] for pp in range(2)]
    QL = [kqb[:, 9216 + pp * 512: 9216 + (pp + 1) * 512] for pp in range(2)]
    EVH = [evb[:, i * 512:(i + 1) * 512] for i in range(2)]
    EVL = [evb[:, 1024 + i * 512: 1024 + (i + 1) * 512] for i in range(2)]
    X8 = XSH
    W8 = WSH

    MROW = STT[0:1, 0:2048]
    YROW = STT[0:1, 2048:4096]
    SCR = lambda a, b: STT[0:1, 4096 + a: 4096 + b]
    AMP = STT[:, 5120:5152]
    ACOL = STT[:, 5152:5153]
    WAMP = STT[:, 5160:5168]
    WCOL = STT[:, 5168:5169]

    pb = [nc.alloc_psum_tensor("pb%d" % i, [128, 512], F32) for i in range(8)]

    sc = Sched()
    bankF = [None] * 8

    def recip_op(out_ap, in_ap, waits=None):
        sc.op("vector", _drain)
        sc.op("vector", lambda e, o=out_ap, i=in_ap: e.reciprocal(o, i), waits=waits)
        return sc.op("vector", lambda e: e.drain(), inc="ve")

    def vop_d(fn, waits=None):
        sc.op("vector", fn, waits=waits)
        return sc.op("vector", _hard, inc="ve")

    sc.op("vector", lambda e: e.memset(ones128[:], 1.0))
    sc.op("vector", lambda e: e.memset(ones1[:], 1.0))
    sc.op("vector", lambda e: e.memset(ccst[:], 0.0))
    sc.op("vector", lambda e: e.memset(smal[:], 0.0))
    t_consts = sc.op("vector", _drain, inc="ve")
    sc.op("sync", lambda e: e.dma_start(out=ident[:], in_=ident_d), inc="ld", amt=16)
    sc.op("sync", lambda e: e.dma_start(out=masks[:], in_=mask_d), inc="ld", amt=16)
    for nm in ["bq", "bk", "bv", "bo", "bf"]:
        sc.op("sync", lambda e, nm=nm: e.dma_start(
            out=bias[nm][:, :], in_=bdr[nm]), inc="ld", amt=16)
    t_const_ld = sc.tok("ld")

    wringF = [None] * 4
    g_chunk = [0]

    def ring_dma(wnm, c, extra=None):
        slot = g_chunk[0] % 4
        tk = sc.op("sync", lambda e, wnm=wnm, c=c, slot=slot: e.dma_start(
            out=WRING[slot][:], in_=wT[wnm][c * 128:(c + 1) * 128, :]),
            waits=wmax(wringF[slot], extra), inc="lw", amt=16)
        g_chunk[0] += 1
        return slot, tk

    evqF = [None, None]
    evbF = [None, None]
    wspF = [None] * CH
    xslF = [None] * CH
    xsetF = [[None] * CH, [None] * CH]
    t_xn_done = [None]
    XSHD = [XSH, XSH2]
    XSLOD = [XSLO, XSLO2]
    st = {}

    def stats_a(tc):
        c0, c1 = tc * 512, (tc + 1) * 512
        xs_tok = [None] * CH
        for c in range(CH):
            xs_tok[c] = sc.op("sync", lambda e, c=c, c0=c0, c1=c1: e.dma_start(
                out=XSL[c][:], in_=xT[c * 128:(c + 1) * 128, c0:c1]),
                waits=wmax(xslF[c]), inc="ld", amt=16)
        st[tc] = {"xs": xs_tok}

    def stats_b(tc):
        c0, c1 = tc * 512, (tc + 1) * 512
        xs_tok = st[tc]["xs"]
        sq_tok = [None] * CH
        sq_mm = [None] * CH
        for c in range(CH):
            sc.op("scalar", lambda e, c=c: e.activation(
                X2B[c % 2][:], XSL[c][:], Act.Square),
                waits=wmax(xs_tok[c], t_xn_done[0], sq_mm[c - 2] if c >= 2 else None))
            sq_tok[c] = sc.op("scalar", _drain, inc="se")
            sc.op("tensor", lambda e, c=c, st_=(c == 0), sp_=(c == CH - 1): e.matmul(
                pb[6][0:1, :], ones128[:], XSL[c][:], start=st_, stop=sp_),
                waits=wmax(xs_tok[c], t_consts, bankF[6] if c == 0 else None), inc="pe")
            sq_mm[c] = sc.op("tensor", lambda e, c=c, st_=(c == 0), sp_=(c == CH - 1): e.matmul(
                pb[7][0:1, :], ones128[:], X2B[c % 2][:], start=st_, stop=sp_),
                waits=wmax(sq_tok[c], bankF[7] if c == 0 else None), inc="pe")
        t_sums = sq_mm[CH - 1]
        mrow, yrow = MROW[0:1, c0:c1], YROW[0:1, c0:c1]
        e2, mmv = SCR(0, 512), SCR(512, 1024)
        sc.op("vector", lambda e, mrow=mrow: e.tensor_scalar(
            mrow, pb[6][0:1, :], 1.0 / 1024.0, None, op0=Alu.mult),
            waits=wmax(t_sums))
        sc.op("vector", _drain, inc="ve")
        tb = sc.op("vector", lambda e, e2=e2: e.tensor_scalar(
            e2, pb[7][0:1, :], 1.0 / 1024.0, None, op0=Alu.mult), inc="ve")
        bankF[6] = bankF[7] = tb
        sc.op("vector", lambda e, mmv=mmv, mrow=mrow: e.tensor_tensor(
            mmv, mrow, mrow, op=Alu.mult), inc="ve")
        sc.op("vector", lambda e, e2=e2, mmv=mmv: e.tensor_tensor(
            e2, e2, mmv, op=Alu.subtract), inc="ve")
        sc.op("vector", lambda e, e2=e2: e.tensor_scalar(
            e2, e2, 1e-5, None, op0=Alu.add), inc="ve")
        t_var = recip_op(mmv, e2)
        sc.op("scalar", lambda e, yrow=yrow, mmv=mmv: e.activation(
            yrow, mmv, Act.Sqrt), waits=wmax(t_var))
        st[tc]["t_y"] = sc.op("scalar", _hard, inc="se")
        st[tc]["sq_last"] = sq_tok[CH - 1]

    def stats_c(tc):
        c0, c1 = tc * 512, (tc + 1) * 512
        s = tc % 2
        mrow, yrow = MROW[0:1, c0:c1], YROW[0:1, c0:c1]
        sc.op("tensor", lambda e, mrow=mrow: e.matmul(
            pb[4][:, :], ones1[:], mrow, start=True, stop=True),
            waits=wmax(st[tc]["t_y"], t_const_ld, bankF[4]), inc="pe")
        t_yb = sc.op("tensor", lambda e, yrow=yrow: e.matmul(
            pb[5][:, :], ones1[:], yrow, start=True, stop=True),
            waits=wmax(bankF[5]), inc="pe")
        xnl_tok = [None] * CH
        prev_hi = None
        for c in range(CH):
            sc.op("vector", lambda e, c=c: e.tensor_tensor(
                X2B[c % 2][:], XSL[c][:], pb[4][:, :], op=Alu.subtract),
                waits=wmax(t_yb, st[tc]["sq_last"]), inc="ve")
            t_xn = sc.op("vector", lambda e, c=c: e.tensor_tensor(
                XNT[:], X2B[c % 2][:], pb[5][:, :], op=Alu.mult),
                waits=wmax(prev_hi, xsetF[s][c]), inc="ve")
            sc.op("vector", lambda e, c=c, tc=tc: e.tensor_reduce(
                AMP[:, tc * 8 + c: tc * 8 + c + 1], XNT[:], axis=AX,
                op=Alu.max, apply_absolute_value=True), inc="ve")
            sc.op("scalar", lambda e, c=c, s=s: e.copy(
                XSHD[s][c], XNT[:]), waits=wmax(t_xn))
            prev_hi = sc.op("scalar", _drain, inc="se")
            sc.op("vector", lambda e, c=c, s=s: e.tensor_tensor(
                XSLOD[s][c], XNT[:], XSHD[s][c], op=Alu.subtract),
                waits=wmax(prev_hi))
            xnl_tok[c] = sc.op("vector", _drain, inc="ve")
            xslF[c] = t_xn
        st[tc]["xnl"] = xnl_tok
        t_xn_done[0] = sc.tok("ve")
        bankF[4] = bankF[5] = t_xn_done[0]

    def emit_wsplit(wnm, c):
        slot, ctk = ring_dma(wnm, c)
        sc.op("scalar", lambda e, c=c, slot=slot: e.copy(
            WSH[c], WRING[slot][:]), waits=wmax(ctk, wspF[c]))
        wh = sc.op("scalar", _drain, inc="se")
        sc.op("vector", lambda e, c=c, slot=slot: e.tensor_tensor(
            WSLO[c], WRING[slot][:], WSH[c], op=Alu.subtract),
            waits=wmax(wh))
        t = sc.op("vector", _drain, inc="ve")
        wringF[slot] = t
        return t

    pend_wl = [None]

    def emit_proj(tc):
        c0, c1 = tc * 512, (tc + 1) * 512
        s = tc % 2
        xnl = st[tc]["xnl"]
        vst_last = [None]
        weights = [("wqT", "bq"), ("wkT", "bk"), ("wvT", "bv")]
        wl_tok = None
        for wi, (wnm, bnm) in enumerate(weights):
            if wi == 0:
                if tc + 1 < TC:
                    stats_a(tc + 1)
                wl_tok = pend_wl[0] if pend_wl[0] is not None else \
                    [emit_wsplit(wnm, c) for c in range(CH)]
            nxt = None
            for half in range(2):
                stopd = {}
                for c in range(CH):
                    for oc4 in range(4):
                        oc = half * 4 + oc4
                        for ps in range(3):
                            ops_ = [(WSH[c], XSHD[s][c]), (WSH[c], XSLOD[s][c]),
                                    (WSLO[c], XSHD[s][c])][ps]
                            tk = sc.op("tensor", lambda e, b=oc4, oc=oc, ops_=ops_,
                                       st_=(c == 0 and ps == 0),
                                       sp_=(c == CH - 1 and ps == 2): e.matmul(
                                pb[b][:, :], ops_[0][:, oc * 128:(oc + 1) * 128], ops_[1],
                                start=st_, stop=sp_),
                                waits=wmax(wl_tok[c], xnl[c],
                                           bankF[oc4] if (c == 0 and ps == 0) else None)
                                if ps == 0 else None, inc="pe")
                            wspF[c] = tk
                            xsetF[s][c] = tk
                        if c == CH - 1:
                            stopd[oc] = tk
                if half == 1:
                    if wi < 2:
                        nxt = [emit_wsplit(weights[wi + 1][0], c) for c in range(CH)]
                    elif tc + 1 < TC:
                        nxt = [emit_wsplit("wqT", c) for c in range(CH)]
                for oc4 in range(4):
                    oc = half * 4 + oc4
                    b = oc4
                    stop = stopd[oc]
                    if wi < 2:
                        dH, dL = (qH_sc, qL_sc) if wi == 0 else (kH_sc, kL_sc)
                        i = oc % 2
                        sc.op("vector", lambda e, b=b, oc=oc, i=i, bnm=bnm: e.tensor_scalar(
                            EVH[i], pb[b][:, :], bias[bnm][:, oc:oc + 1], None, op0=Alu.add),
                            waits=wmax(stop, evbF[i]), inc="ve")
                        sc.op("vector", lambda e, b=b, oc=oc, i=i, bnm=bnm: e.scalar_tensor_tensor(
                            EVL[i], pb[b][:, :], bias[bnm][:, oc:oc + 1], EVH[i],
                            op0=Alu.add, op1=Alu.subtract))
                        tlo = sc.op("vector", _drain, inc="ve")
                        bankF[b] = tlo
                        sc.op("scalar", lambda e, oc=oc, i=i, dH=dH, c0=c0, c1=c1: e.dma_start(
                            out=dH[oc * 128:(oc + 1) * 128, c0:c1], in_=EVH[i]),
                            waits=wmax(tlo), inc="sd", amt=16)
                        evbF[i] = sc.op("scalar", lambda e, oc=oc, i=i, dL=dL, c0=c0, c1=c1: e.dma_start(
                            out=dL[oc * 128:(oc + 1) * 128, c0:c1], in_=EVL[i]),
                            inc="sd", amt=16)
                    else:
                        ev = EVQ[oc % 2]
                        sc.op("vector", lambda e, b=b, oc=oc, ev=ev: e.tensor_scalar(
                            ev[:], pb[b][:, :], bias["bv"][:, oc:oc + 1], None, op0=Alu.add),
                            waits=wmax(stop, evqF[oc % 2]))
                        tkv = sc.op("vector", _drain, inc="ve")
                        tt = None
                        for kk in range(4):
                            tt = sc.op("tensor", lambda e, b=b, ev=ev, kk=kk: e.transpose(
                                pb[b][:, kk * 128:(kk + 1) * 128],
                                ev[:, kk * 128:(kk + 1) * 128], ident[:]),
                                waits=wmax(tkv) if kk == 0 else None, inc="pe")
                        evqF[oc % 2] = tt
                        sc.op("scalar", lambda e, b=b, oc=oc: e.copy(
                            VSTT.rearrange("p (kk j) -> p kk j", j=1024)[:, :, oc * 128:(oc + 1) * 128],
                            pb[b][:, :]), waits=wmax(tt))
                        vst_last[0] = sc.op("scalar", _drain, inc="se")
                        bankF[b] = vst_last[0]
            if wi == 0 and tc + 1 < TC:
                stats_b(tc + 1)
            if wi == 1 and tc + 1 < TC:
                stats_c(tc + 1)
            if nxt is not None and wi < 2:
                wl_tok = nxt
        pend_wl[0] = nxt if tc + 1 < TC else None
        for kk in range(4):
            sc.op("scalar", lambda e, kk=kk, c0=c0: e.dma_start(
                out=v_sc[c0 + kk * 128: c0 + (kk + 1) * 128, :], in_=VST[kk][:]),
                waits=wmax(vst_last[0]), inc="sd", amt=16)

    stats_a(0)
    stats_b(0)
    stats_c(0)
    for tc in range(TC):
        emit_proj(tc)

    t_p1_pe, t_p1_ve = sc.tok("pe"), sc.tok("ve")
    t_p1_se, t_p1_sd = sc.tok("se"), sc.tok("sd")

    sc.op("vector", lambda e: e.tensor_reduce(
        ACOL, AMP[:, :], axis=AX, op=Alu.max))
    t_r1 = sc.op("vector", _hard, inc="ve")
    t_tr = sc.op("tensor", lambda e: e.transpose(
        pb[2][0:1, 0:128], ACOL, ident[:]), waits=wmax(t_r1, bankF[2]), inc="pe")
    sc.op("vector", lambda e: e.tensor_reduce(
        ccst[0:1, 0:1], pb[2][0:1, 0:128], axis=AX, op=Alu.max),
        waits=wmax(t_tr))
    t_r2 = sc.op("vector", _hard, inc="ve")
    bankF[2] = t_r2
    t_ccw = sc.op("gpsimd", lambda e: e.dma_start(out=cc_in_t.ap(), in_=ccst[:]),
                  waits=wmax(t_r2), inc="gd", amt=16)
    t_cc = sc.op("gpsimd", lambda e: e.collective_compute(
        "AllReduce", Alu.max, replica_groups=[list(range(NCORE))],
        ins=[cc_in_t.ap().opt()], outs=[cc_out_t.ap().opt()]),
        waits=wmax(t_ccw), inc="cc")
    t_ccr = sc.op("gpsimd", lambda e: e.dma_start(out=ccrd[:], in_=cc_out_t.ap()),
                  waits=wmax(t_cc), inc="gd", amt=16)

    xspF = list(xsetF[0])
    kF = [t_p1_pe, t_p1_pe]
    qF = [t_p1_pe, t_p1_pe]
    vF = [t_p1_pe, t_p1_pe]
    ptF = [t_p1_pe, t_p1_pe]
    otsF = t_p1_pe
    smpF = [t_p1_ve]
    rbF = [t_p1_ve, t_p1_ve]
    oproj_end = t_p1_pe
    ostgF = [None, None]

    RECCF = [None]
    pend = [None]

    def emit_sums_flush():
        if pend[0] is None:
            return
        p_, t_rc_, pvev_, nkt_ = pend[0]
        pend[0] = None
        t_tr2 = None
        for j_ in range(4):
            t_tr2 = sc.op("tensor", lambda e, j_=j_: e.transpose(
                pb[4][0:1, j_ * 128:(j_ + 1) * 128], RECC[:, j_:j_ + 1], ident[:]),
                waits=wmax(t_rc_, bankF[4]) if j_ == 0 else None, inc="pe")
        for j_ in range(4):
            t_tr2 = sc.op("tensor", lambda e, j_=j_: e.transpose(
                pb[5][0:1, j_ * 128:(j_ + 1) * 128], RECC[:, 4 + j_:5 + j_], ident[:]),
                waits=wmax(bankF[5]) if j_ == 0 else None, inc="pe")
        RECCF[0] = t_tr2
        sc.op("scalar", lambda e: e.copy(RRW[0:1, 0:512], pb[4][0:1, 0:512]),
              waits=wmax(t_tr2))
        sc.op("scalar", lambda e: e.copy(RRW[0:1, 512:1024], pb[5][0:1, 0:512]))
        t_e3 = sc.op("scalar", _hard, inc="se")
        t_b0 = sc.op("tensor", lambda e: e.matmul(
            pb[4][:, :], ones1[:], RRW[0:1, 0:512], start=True, stop=True),
            waits=wmax(t_e3), inc="pe")
        t_b1 = sc.op("tensor", lambda e: e.matmul(
            pb[5][:, :], ones1[:], RRW[0:1, 512:1024], start=True, stop=True),
            inc="pe")
        t_rb0 = sc.op("scalar", lambda e: e.copy(RB[0][:], pb[4][:, :]),
                      waits=wmax(t_b0, rbF[0]), inc="se")
        sc.op("scalar", lambda e: e.copy(RB[1][:], pb[5][:, :]),
              waits=wmax(t_b1, rbF[1]))
        t_rb1 = sc.op("scalar", _drain, inc="se")
        bankF[4] = t_rb0
        bankF[5] = t_rb1
        t_n0 = sc.op("vector", lambda e, p_=p_: e.tensor_tensor(
            OTS[p_][0:64, :], OTS[p_][0:64, :], RB[0][0:64, :], op=Alu.mult),
            waits=wmax(t_rb1, pvev_), inc="ve")
        sc.op("vector", lambda e, p_=p_: e.tensor_tensor(
            OTS[p_][64:128, :], OTS[p_][64:128, :], RB[1][64:128, :], op=Alu.mult))
        t_n1 = sc.op("vector", _drain, inc="ve")
        rbF[0] = t_n0
        rbF[1] = t_n1
        sc.op("scalar", lambda e, p_=p_: e.copy(
            XSH[p_], OTS[p_][:, :]), waits=wmax(t_n1, xspF[p_]))
        oh = sc.op("scalar", _drain, inc="se")
        sc.op("vector", lambda e, p_=p_: e.tensor_tensor(
            XSLO[p_], OTS[p_][:, :], XSH[p_], op=Alu.subtract),
            waits=wmax(oh))
        ol_tok[p_] = sc.op("vector", _drain, inc="ve")

    ol_tok = [None] * CH
    wo_tok = [None] * CH
    wam = {}
    for qt in range(TC):
        q0, q1 = qt * 512, (qt + 1) * 512
        nkt = qt + 1
        nkc = nkt * 4
        for p in range(PAIRS):
            pp = p % 2
            tq1 = sc.op("sync", lambda e, p=p, q0=q0, q1=q1, pp=pp: e.dma_start(
                out=QH[pp][:, :], in_=qH_sc[p * 128:(p + 1) * 128, q0:q1]),
                waits=wmax(t_p1_sd, qF[pp]), inc="lq", amt=16)
            tq2 = sc.op("sync", lambda e, p=p, q0=q0, q1=q1, pp=pp: e.dma_start(
                out=QL[pp][:, :], in_=qL_sc[p * 128:(p + 1) * 128, q0:q1]),
                inc="lq", amt=16)
            tk1 = sc.op("sync", lambda e, p=p, pp=pp, nkt=nkt: e.dma_start(
                out=KH[pp][:, 0:nkt * 512], in_=kH_sc[p * 128:(p + 1) * 128, 0:nkt * 512]),
                waits=wmax(kF[pp]), inc="lq", amt=16)
            tk2 = sc.op("sync", lambda e, p=p, pp=pp, nkt=nkt: e.dma_start(
                out=KL[pp][:, 0:nkt * 512], in_=kL_sc[p * 128:(p + 1) * 128, 0:nkt * 512]),
                inc="lq", amt=16)
            tqk = wmax(tq1, tq2, tk1, tk2)
            t_v = None
            for kc in range(nkc):
                t_v = sc.op("sync", lambda e, p=p, kc=kc, pp=pp: e.dma_start(
                    out=VBUFS[pp][:, kc * 128:(kc + 1) * 128],
                    in_=v_sc[kc * 128:(kc + 1) * 128, p * 128:(p + 1) * 128]),
                    waits=wmax(t_p1_sd, vF[pp]), inc="sv", amt=16)
            exp_tok = {}
            pv_ev = {}
            lastT = [None]

            def emit_S_exp(qc):
                order = [(h, kt) for kt in range(nkt) for h in range(2)]
                s_tok = {}
                for idx, (h, kt) in enumerate(order):
                    b = 2 * ((qc * nkt + kt) % 2) + h
                    t_s = None
                    for ps in range(3):
                        qop = [QH, QH, QL][ps][pp]
                        kop = [KH, KL, KH][ps][pp]
                        t_s = sc.op("tensor",
                                    lambda e, h=h, qc=qc, kt=kt, b=b, ps=ps,
                                    qop=qop, kop=kop: e.matmul(
                            pb[b][:, :],
                            qop[h * 64:(h + 1) * 64, qc * 128:(qc + 1) * 128],
                            kop[h * 64:(h + 1) * 64, kt * 512:(kt + 1) * 512],
                            start=(ps == 0), stop=(ps == 2),
                            tile_position=(h * 64, 0)),
                            waits=wmax(tqk, bankF[b], oproj_end)
                            if ps == 0 else None, inc="pe")
                    s_tok[(h, kt)] = t_s
                    if idx >= 2:
                        emit_exp(qc, *order[idx - 2], s_tok)
                for (h, kt) in order[-2:]:
                    emit_exp(qc, h, kt, s_tok)

            def emit_exp(qc, h, kt, s_tok):
                b = 2 * ((qc * nkt + kt) % 2) + h
                wexp = wmax(s_tok[(h, kt)], smpF[0], lastT[0])
                if kt == qt:
                    sc.op("vector", lambda e, b=b, qc=qc: e.tensor_tensor(
                        pb[b][:, :], pb[b][:, :],
                        masks[:, qc * 512:(qc + 1) * 512], op=Alu.add),
                        waits=wmax(s_tok[(h, kt)]))
                    t_m = sc.op("vector", _drain, inc="ve")
                    wexp = wmax(t_m, smpF[0], lastT[0])
                col = h * 64 + qc * 16 + kt
                sc.op("scalar", lambda e, h=h, kt=kt, b=b, col=col: e.activation(
                    PTIL[h][:, kt * 512:(kt + 1) * 512], pb[b][:, :], Act.Exp,
                    bias=0.0, scale=1.0 / 32.0,
                    accum_out=SMP[:, col:col + 1]), waits=wexp)
                te = sc.op("scalar", _drain, inc="se")
                bankF[b] = te
                exp_tok[(qc, h, kt)] = te

            def emit_tpv(qc):
                tpF = {}
                tt = None
                for h in range(2):
                    for g in range(nkt):
                        tbk = 4 + ((h * nkt + g) % 2)
                        for i in range(4):
                            kc = g * 4 + i
                            tt = sc.op("tensor", lambda e, h=h, kc=kc, i=i, tbk=tbk: e.transpose(
                                pb[tbk][:, i * 128:(i + 1) * 128],
                                PTIL[h][:, kc * 128:(kc + 1) * 128], ident[:]),
                                waits=wmax(exp_tok[(qc, h, g)], bankF[tbk],
                                           ptF[h] if g == 0 else None)
                                if i == 0 else None, inc="pe")
                        eng, sem = "vector", "ve"
                        sc.op(eng, lambda e, h=h, g=g, tbk=tbk: e.tensor_copy(
                            PT[h][:, g * 512:(g + 1) * 512], pb[tbk][:, :]),
                            waits=wmax(tt))
                        tev = sc.op(eng, _drain, inc=sem)
                        bankF[tbk] = tev
                        tpF[h] = tev
                lastT[0] = tt
                tpv = {}
                for kc in range(nkc):
                    tpv[0] = sc.op("tensor", lambda e, kc=kc, nkc=nkc, pp=pp: e.matmul(
                        pb[6][0:64, 0:128], VBUFS[pp][:, kc * 128: kc * 128 + 64],
                        PT[0][:, kc * 128:(kc + 1) * 128],
                        start=(kc == 0), stop=(kc == nkc - 1), tile_position=(0, 0)),
                        waits=wmax(t_v, tpF[0], bankF[6]) if kc == 0 else None,
                        inc="pe")
                    tpv[1] = sc.op("tensor", lambda e, kc=kc, nkc=nkc, pp=pp: e.matmul(
                        pb[7][64:128, 0:128], VBUFS[pp][:, kc * 128 + 64: kc * 128 + 128],
                        PT[1][:, kc * 128:(kc + 1) * 128],
                        start=(kc == 0), stop=(kc == nkc - 1), tile_position=(0, 64)),
                        waits=wmax(t_v, tpF[1], bankF[7]) if kc == 0 else None,
                        inc="pe")
                t_e0 = sc.op("vector", lambda e, p=p, qc=qc: e.tensor_copy(
                    OTS[p][0:64, qc * 128:(qc + 1) * 128], pb[6][0:64, 0:128]),
                    waits=wmax(tpv[0], otsF if qc == 0 else None,
                               t_p1_sd if qc == 0 else None), inc="ve")
                sc.op("vector", lambda e, p=p, qc=qc: e.tensor_copy(
                    OTS[p][64:128, qc * 128:(qc + 1) * 128], pb[7][64:128, 0:128]),
                    waits=wmax(tpv[1]))
                t_e1 = sc.op("vector", _drain, inc="ve")
                bankF[6] = t_e0
                bankF[7] = t_e1
                pv_ev[qc] = t_e1

            emit_S_exp(0)
            emit_sums_flush()
            for qc in range(1, 4):
                emit_tpv(qc - 1)
                emit_S_exp(qc)
            t_slast = sc.tok("pe")
            emit_tpv(3)
            t_rd1 = None
            for h in range(2):
                for qc in range(4):
                    cb = h * 64 + qc * 16
                    t_rd1 = sc.op("vector", lambda e, h=h, qc=qc, cb=cb, nkt=nkt: e.tensor_reduce(
                        SUMC[:, h * 4 + qc: h * 4 + qc + 1], SMP[:, cb:cb + nkt],
                        axis=AX, op=Alu.add),
                        waits=wmax(exp_tok[(qc, h, qt)]), inc="ve")
            smpF[0] = t_rd1
            t_rc = recip_op(RECC[:, 0:8], SUMC[:, 0:8], waits=wmax(t_rd1, RECCF[0]))
            pend[0] = (p, t_rc, pv_ev[3], nkt)
            end_pe = sc.tok("pe")
            kF[pp] = t_slast
            qF[pp] = t_slast
            vF[pp] = end_pe
            ptF = [sc.tok("pe"), sc.tok("pe")]
        t_pairs_pe = sc.tok("pe")
        if qt == TC - 1:
            t_pt_se = sc.tok("se")
            t_wred = None
            for c_ in range(CH):
                slot_, ctk_ = ring_dma("wfT", c_, extra=wmax(t_pairs_pe, t_pt_se))
                sc.op("vector", lambda e, c_=c_, slot_=slot_: e.tensor_reduce(
                    WAMP[:, c_:c_ + 1], WRING[slot_][:], axis=AX, op=Alu.max,
                    apply_absolute_value=True), waits=wmax(ctk_))
                t_wred = sc.op("vector", _hard, inc="ve")
                wringF[slot_] = t_wred
            sc.op("vector", lambda e: e.tensor_reduce(
                WCOL, WAMP[:, 0:8], axis=AX, op=Alu.max))
            wam["t_wc"] = sc.op("vector", _hard, inc="ve")
        if qt == 0:
            for s in range(4):
                wringF[s] = wmax(wringF[s], t_pairs_pe)
            for c in range(CH):
                slot, ctk = ring_dma("woT", c)
                sc.op("scalar", lambda e, c=c, slot=slot: e.copy(
                    WSH[c], WRING[slot][:]), waits=wmax(ctk, wspF[c]))
                wh = sc.op("scalar", _drain, inc="se")
                sc.op("vector", lambda e, c=c, slot=slot: e.tensor_tensor(
                    WSLO[c], WRING[slot][:], WSH[c], op=Alu.subtract),
                    waits=wmax(wh))
                wo_tok[c] = sc.op("vector", _drain, inc="ve")
                wringF[slot] = wo_tok[c]
        wl_tok = wo_tok
        stop_tok = [None] * CH

        def oproj_mm(c, oc):
            tkk = None
            for ps in range(3):
                ops_ = [(WSH[c], XSH[c]), (WSH[c], XSLO[c]), (WSLO[c], XSH[c])][ps]
                tkk = sc.op("tensor", lambda e, c=c, oc=oc, ops_=ops_,
                            st_=(c == 0 and ps == 0),
                            sp_=(c == CH - 1 and ps == 2): e.matmul(
                    pb[oc][:, :], ops_[0][:, oc * 128:(oc + 1) * 128], ops_[1],
                    start=st_, stop=sp_),
                    waits=wmax(wl_tok[c], ol_tok[c],
                               bankF[oc] if (c == 0 and ps == 0) else None)
                    if ps == 0 else None, inc="pe")
            if c == CH - 1:
                stop_tok[oc] = tkk
            wspF[c] = tkk
            xspF[c] = tkk

        def oproj_evac(oc):
            i = oc % 2
            sc.op("vector", lambda e, oc=oc, i=i: e.tensor_scalar(
                OSTG[i][:], pb[oc][:, :], bias["bo"][:, oc:oc + 1], None, op0=Alu.add),
                waits=wmax(stop_tok[oc], ostgF[i]))
            tev = sc.op("vector", _drain, inc="ve")
            bankF[oc] = tev
            ostgF[i] = sc.op("scalar", lambda e, oc=oc, i=i, q0=q0, q1=q1: e.dma_start(
                out=att_sc[oc * 128:(oc + 1) * 128, q0:q1], in_=OSTG[i][:]),
                waits=wmax(tev), inc="sg", amt=16)

        for c in range(CH - 1):
            for oc in range(4):
                oproj_mm(c, oc)
        emit_sums_flush()
        for oc in range(4):
            oproj_mm(CH - 1, oc)
        for oc in range(4):
            oproj_evac(oc)
        for c in range(CH):
            for oc in range(4, CH):
                oproj_mm(c, oc)
        for oc in range(4, CH):
            oproj_evac(oc)
        oproj_end = sc.tok("pe")
        otsF = oproj_end

    t_p3_pe, t_p3_ve, t_p3_se = sc.tok("pe"), sc.tok("ve"), sc.tok("se")
    t_p3_sd = sc.tok("sd")
    t_p3_sg = sc.tok("sg")

    t_wtr = sc.op("tensor", lambda e: e.transpose(
        pb[2][0:1, 0:128], WCOL, ident[:]), waits=wmax(wam["t_wc"], bankF[2]), inc="pe")
    sc.op("vector", lambda e: e.tensor_reduce(
        smal[0:1, 2:3], pb[2][0:1, 0:128], axis=AX, op=Alu.max),
        waits=wmax(t_wtr))
    t_wamax = sc.op("vector", _hard, inc="ve")
    bankF[2] = t_wamax
    vop_d(lambda e: e.tensor_scalar(
        smal[0:1, 4:5], ccrd[0:1, 0:1], 1e-12, None, op0=Alu.max),
        waits=wmax(t_ccr, t_wamax))
    recip_op(smal[0:1, 6:7], smal[0:1, 4:5])
    vop_d(lambda e: e.tensor_scalar(
        smal[0:1, 6:7], smal[0:1, 6:7], 448.0, 448.0, op0=Alu.mult, op1=Alu.min))
    recip_op(smal[0:1, 7:8], smal[0:1, 6:7])
    vop_d(lambda e: e.tensor_scalar(
        smal[0:1, 9:10], smal[0:1, 2:3], 1e-12, None, op0=Alu.max))
    recip_op(smal[0:1, 10:11], smal[0:1, 9:10])
    vop_d(lambda e: e.tensor_scalar(
        smal[0:1, 10:11], smal[0:1, 10:11], 448.0, 448.0, op0=Alu.mult, op1=Alu.min))
    recip_op(smal[0:1, 11:12], smal[0:1, 10:11])
    t_cmul = vop_d(lambda e: e.tensor_tensor(
        smal[0:1, 12:13], smal[0:1, 7:8], smal[0:1, 11:12], op=Alu.mult))
    bc_done = None
    prev_bc = None
    for src, dst in [(12, 0), (6, 1), (10, 2)]:
        tbc = sc.op("tensor", lambda e, src=src: e.matmul(
            pb[3][:, 0:1], ones1[:], smal[0:1, src:src + 1], start=True, stop=True),
            waits=wmax(t_cmul, bankF[3], prev_bc), inc="pe")
        sc.op("vector", lambda e, dst=dst: e.tensor_copy(
            scl[:, dst:dst + 1], pb[3][:, 0:1]), waits=wmax(tbc))
        bc_done = sc.op("vector", _hard, inc="ve")
        prev_bc = bc_done
        bankF[3] = bc_done

    QSET = [
        {"t0": QTMP[0], "t1": QTMP[1], "q8": qt8,
         "ob": qtb[:, 0:512], "mk": qmsk[:, 0:512]},
        {"t0": STT[:, 4096:4608], "t1": STT[:, 4608:5120], "q8": qt8b,
         "ob": qtbB[:, 0:512], "mk": qmskb[:, 0:512]},
    ]
    lastq = [{"se": None, "ve": None}, {"se": None, "ve": None}]

    def quant_q(src_ap, dst_ap, scale_col, qs, extra=None):
        B, L = QSET[qs], lastq[qs]
        t_in = sc.op("vector", lambda e, B=B, src_ap=src_ap, scale_col=scale_col:
                     e.tensor_scalar(B["t0"][:], src_ap, scl[:, scale_col:scale_col + 1],
                                     448.0, op0=Alu.mult, op1=Alu.min),
                     waits=wmax(bc_done, L["ve"], L["se"], extra), inc="ve")
        sc.op("vector", lambda e, B=B: e.tensor_scalar(
            B["t0"][:], B["t0"][:], -448.0, None, op0=Alu.max), inc="ve")
        sc.op("vector", lambda e, B=B: e.tensor_copy(B["q8"][:, 0:512], B["t0"][:]))
        t_8a = sc.op("vector", _drain, inc="ve")
        sc.op("vector", lambda e, B=B: e.tensor_scalar(
            B["t1"][:], B["t0"][:], 0.5, None, op0=Alu.mult), inc="ve")
        sc.op("vector", lambda e, B=B: e.tensor_copy(B["q8"][:, 512:1024], B["t1"][:]))
        t_8b = sc.op("vector", _drain, inc="ve")
        sc.op("scalar", lambda e, B=B, dst_ap=dst_ap: e.copy(dst_ap, B["q8"][:, 0:512]),
              waits=wmax(t_8a, L["se"]))
        t_da = sc.op("scalar", _drain, inc="se")
        sc.op("scalar", lambda e, B=B: e.mul(B["ob"], B["q8"][:, 512:1024], 2.0),
              waits=wmax(t_8b))
        sc.op("scalar", lambda e, B=B: e.activation(B["t1"][:], B["t0"][:], Act.Abs))
        t_ob = sc.op("scalar", _drain, inc="se")
        sc.op("vector", lambda e, B=B: e.tensor_scalar(
            B["mk"], B["t1"][:], 240.0, None, op0=Alu.is_ge),
            waits=wmax(t_ob), inc="ve")
        sc.op("vector", lambda e, B=B, dst_ap=dst_ap: e.copy_predicated(
            dst_ap, B["mk"], B["ob"]), waits=wmax(t_da))
        t_cp = sc.op("vector", _drain, inc="ve")
        L["se"] = t_ob
        L["ve"] = t_cp
        return t_cp, t_in

    w8_tok = None
    for c in range(CH):
        slot, ctk = ring_dma("wfT", c)
        rdF = None
        for half in range(2):
            w8_tok, t_rd = quant_q(WRING[slot][:, half * 512:(half + 1) * 512],
                                   W8[c][:, half * 512:(half + 1) * 512], 2, half,
                                   extra=wmax(ctk, wspF[c] if half == 0 else None))
            rdF = wmax(rdF, t_rd)
        wringF[slot] = rdF
    t_w8nop = sc.op("tensor", lambda e: e.nop(),
                    waits=wmax(w8_tok, lastq[0]["ve"], lastq[1]["ve"]), inc="pe")

    xreF = [None] * CH
    xn2F = [None] * CH
    x8F = [None] * CH
    attF = [None, None]
    ffbF = [None, None]
    ffsF = [None] * 4
    for tc in range(TC):
        c0, c1 = tc * 512, (tc + 1) * 512
        xr_tok = [None] * CH
        for c in range(CH):
            xr_tok[c] = sc.op("sync", lambda e, c=c, c0=c0, c1=c1: e.dma_start(
                out=XRE[c][:], in_=xT[c * 128:(c + 1) * 128, c0:c1]),
                waits=wmax(xreF[c], t_p3_pe if tc == 0 else None), inc="ld", amt=16)
        mrow, yrow = MROW[0:1, c0:c1], YROW[0:1, c0:c1]
        sc.op("tensor", lambda e, mrow=mrow: e.matmul(
            pb[4][:, :], ones1[:], mrow, start=True, stop=True),
            waits=wmax(bankF[4], t_w8nop), inc="pe")
        t_yb = sc.op("tensor", lambda e, yrow=yrow: e.matmul(
            pb[5][:, :], ones1[:], yrow, start=True, stop=True),
            waits=wmax(bankF[5]), inc="pe")
        xn_t = [None] * CH
        for c in range(CH):
            sc.op("vector", lambda e, c=c: e.tensor_tensor(
                XN2[c][:], XRE[c][:], pb[4][:, :], op=Alu.subtract),
                waits=wmax(t_yb, xr_tok[c], xn2F[c], lastq[0]["ve"], lastq[1]["ve"],
                           t_p3_ve if tc == 0 and c == 0 else None), inc="ve")
            xn_t[c] = sc.op("vector", lambda e, c=c: e.tensor_tensor(
                XN2[c][:], XN2[c][:], pb[5][:, :], op=Alu.mult), inc="ve")
        bankF[4] = sc.tok("ve")
        bankF[5] = sc.tok("ve")
        x8_t = [None] * CH
        for c in range(CH):
            x8_t[c], _ = quant_q(XN2[c][:], X8[c][:], 1, c % 2,
                                 extra=wmax(xn_t[c], x8F[c],
                                            xspF[c] if tc == 0 else None))
        stop_tok = [None] * CH
        for c in range(CH):
            for oc in range(CH):
                tkk = sc.op("tensor", lambda e, c=c, oc=oc: e.matmul(
                    pb[oc][:, :], W8[c][:, oc * 128:(oc + 1) * 128], X8[c][:],
                    start=(c == 0), stop=(c == CH - 1)),
                    waits=wmax(x8_t[c], bankF[oc] if c == 0 else None,
                               t_p3_pe if tc == 0 and c == 0 else None), inc="pe")
                if c == CH - 1:
                    stop_tok[oc] = tkk
        for c in range(CH):
            x8F[c] = stop_tok[CH - 1]
            xn2F[c] = xn_t[c]
        for oc in range(CH):
            fb = oc % 2
            ta = sc.op("scalar", lambda e, oc=oc, fb=fb, c0=c0, c1=c1: e.dma_start(
                out=ATT[fb][:], in_=att_sc[oc * 128:(oc + 1) * 128, c0:c1]),
                waits=wmax(attF[fb], t_p3_sg), inc="sg", amt=16)
            t_fe = sc.op("vector", lambda e, oc=oc, fb=fb: e.tensor_scalar(
                ffb[:, fb * 512:(fb + 1) * 512], pb[oc][:, :],
                scl[:, 0:1], bias["bf"][:, oc:oc + 1], op0=Alu.mult, op1=Alu.add),
                waits=wmax(stop_tok[oc], ffbF[fb]), inc="ve")
            bankF[oc] = t_fe
            s0, s1 = fb * 2, fb * 2 + 1
            t_a1 = sc.op("vector", lambda e, oc=oc, fb=fb, s0=s0: e.tensor_tensor(
                FFS[s0][:], XRE[oc][:], ATT[fb][:], op=Alu.add),
                waits=wmax(xr_tok[oc], ta, ffsF[s0]), inc="ve")
            attF[fb] = t_a1
            sc.op("vector", lambda e, s0=s0, s1=s1, fb=fb: e.tensor_tensor(
                FFS[s1][:], FFS[s0][:], ffb[:, fb * 512:(fb + 1) * 512],
                op=Alu.add), waits=wmax(t_fe, ffsF[s1]))
            t_a2 = sc.op("vector", _drain, inc="ve")
            ffbF[fb] = t_a2
            ffsF[s0] = t_a2
            xreF[oc] = t_a2
            t_out = sc.op("scalar", lambda e, oc=oc, c0=c0, c1=c1, s1=s1: e.dma_start(
                out=outT[oc * 128:(oc + 1) * 128, c0:c1], in_=FFS[s1][:]),
                waits=wmax(t_a2), inc="so", amt=16)
            ffsF[s1] = t_out

    if DBG:
        dqh = nc.dram_tensor("dqh", [D, T], BF16, kind="ExternalOutput").ap()
        dql = nc.dram_tensor("dql", [D, T], BF16, kind="ExternalOutput").ap()
        dkh = nc.dram_tensor("dkh", [D, T], BF16, kind="ExternalOutput").ap()
        dkl = nc.dram_tensor("dkl", [D, T], BF16, kind="ExternalOutput").ap()
        dv = nc.dram_tensor("dv", [T, D], F32, kind="ExternalOutput").ap()
        datt = nc.dram_tensor("datt", [D, T], F32, kind="ExternalOutput").ap()
        dsml = nc.dram_tensor("dsml", [1, 128], F32, kind="ExternalOutput").ap()
        dstt = nc.dram_tensor("dstt", [128, 5184], F32, kind="ExternalOutput").ap()
        t_all = wmax(sc.tok("ve"), sc.tok("se"), sc.tok("pe"), sc.tok("sd"), sc.tok("so"))
        sc.op("gpsimd", lambda e: e.dma_start(out=dqh, in_=qH_sc), waits=t_all, inc="so", amt=16)
        sc.op("gpsimd", lambda e: e.dma_start(out=dql, in_=qL_sc), inc="so", amt=16)
        sc.op("gpsimd", lambda e: e.dma_start(out=dkh, in_=kH_sc), inc="so", amt=16)
        sc.op("gpsimd", lambda e: e.dma_start(out=dkl, in_=kL_sc), inc="so", amt=16)
        sc.op("gpsimd", lambda e: e.dma_start(out=dv, in_=v_sc), inc="so", amt=16)
        sc.op("gpsimd", lambda e: e.dma_start(out=datt, in_=att_sc), inc="so", amt=16)
        sc.op("gpsimd", lambda e: e.dma_start(out=dsml, in_=smal[:]), inc="so", amt=16)
        sc.op("gpsimd", lambda e: e.dma_start(out=dstt, in_=STT[:, :]), inc="so", amt=16)
        dx8 = nc.dram_tensor("dx8", [128, 8192], BF16, kind="ExternalOutput").ap()
        dw8 = nc.dram_tensor("dw8", [128, 16384], BF16, kind="ExternalOutput").ap()
        sc.op("gpsimd", lambda e: e.dma_start(out=dx8, in_=xnspl[:, :]), inc="so", amt=16)
        sc.op("gpsimd", lambda e: e.dma_start(out=dw8, in_=wspl[:, :]), inc="so", amt=16)
    sc.op("gpsimd", lambda e: e.nop(), waits=wmax(sc.tok("so")))

    sems = {s: nc.alloc_semaphore(s) for s in
            ["ld", "sd", "gd", "pe", "ve", "se", "cc", "so",
             "lw", "lq", "lr", "sv", "sg"]}
    DMA_SEMS = {"ld", "sd", "gd", "so", "lw", "lq", "lr", "sv", "sg"}
    sc_counts = {}
    with nc.Block() as block:
        def mk(engname):
            def body(eng):
                seen = {}
                prev_dma = {}
                last_ins = [None]
                for (e, fn, waits, inc, amt) in sc.ops:
                    if e != engname:
                        continue
                    if fn is _drain and not FENCE and inc is not None:
                        if last_ins[0] is not None:
                            last_ins[0].then_inc(sems[inc], amt)
                            continue
                    w = dict(waits)
                    if inc in DMA_SEMS:
                        pv = prev_dma.get(inc, 0)
                        if pv and w.get(inc, 0) < pv:
                            w[inc] = pv
                    for s, v in w.items():
                        if seen.get(s, 0) < v:
                            eng.wait_ge(sems[s], v)
                            seen[s] = v
                    ins = fn(eng)
                    if ins is not None:
                        last_ins[0] = ins
                    if inc is not None and ins is not None:
                        ins.then_inc(sems[inc], amt)
                        if inc in DMA_SEMS:
                            prev_dma[inc] = prev_dma.get(inc, 0) + amt
            return body
        block.sync(mk("sync"))
        block.scalar(mk("scalar"))
        block.vector(mk("vector"))
        block.tensor(mk("tensor"))
        block.gpsimd(mk("gpsimd"))
    return nc


_CACHE = {}


def _get_nc():
    if "nc" not in _CACHE:
        _CACHE["nc"] = build_program()
    return _CACHE["nc"]


def kernel(L0_attn_x, L0_attn_Wq_w, L0_attn_Wq_b, L0_attn_Wk_w, L0_attn_Wk_b,
           L0_attn_Wv_w, L0_attn_Wv_b, L0_attn_Wo_w, L0_attn_Wo_b,
           L0_ffn_weight, L0_ffn_bias):
    import ml_dtypes
    x = np.ascontiguousarray(np.asarray(L0_attn_x, dtype=np.float32))
    ws = {
        "wqT": np.ascontiguousarray(np.asarray(L0_attn_Wq_w, dtype=np.float32).T),
        "wkT": np.ascontiguousarray(np.asarray(L0_attn_Wk_w, dtype=np.float32).T),
        "wvT": np.ascontiguousarray(np.asarray(L0_attn_Wv_w, dtype=np.float32).T),
        "woT": np.ascontiguousarray(np.asarray(L0_attn_Wo_w, dtype=np.float32).T),
        "wfT": np.ascontiguousarray(np.asarray(L0_ffn_weight, dtype=np.float32).T),
    }

    def _btile(b):
        return np.ascontiguousarray(
            np.asarray(b, dtype=np.float32).reshape(8, 128).T)
    bs = {
        "bq": _btile(L0_attn_Wq_b), "bk": _btile(L0_attn_Wk_b),
        "bv": _btile(L0_attn_Wv_b), "bo": _btile(L0_attn_Wo_b),
        "bf": _btile(L0_ffn_bias),
    }
    ident = np.eye(128, dtype=np.float32)
    i_ = np.arange(128)[:, None]
    j_ = np.arange(512)[None, :]
    maskadd = np.concatenate(
        [np.where(o * 128 + i_ >= j_, 0.0, NEG).astype(np.float32)
         for o in range(4)], axis=1).astype(ml_dtypes.bfloat16)

    nc = _get_nc()
    in_maps = []
    for b in range(NCORE):
        im = {"xT": np.ascontiguousarray(x[b].T), "ident": ident, "maskadd": maskadd}
        im.update(ws)
        im.update(bs)
        in_maps.append(im)
    res = run_bass_kernel_spmd(nc, in_maps, core_ids=list(range(NCORE)))
    out = np.stack([np.asarray(res.results[b]["outT"]).T for b in range(NCORE)])
    return np.ascontiguousarray(out).astype(np.float32, copy=False)


# version 37 (final)
# speedup vs baseline: 1.0192x; 1.0161x over previous
import numpy as np
import concourse.bass as bass
import concourse.mybir as mybir
from concourse.bass_utils import run_bass_kernel_spmd

F32 = mybir.dt.float32
BF16 = mybir.dt.bfloat16
FP8 = mybir.dt.float8e4
U8 = mybir.dt.uint8
Alu = mybir.AluOpType
Act = mybir.ActivationFunctionType
AX = mybir.AxisListType.X

T, D, CH, TC, PAIRS, NCORE = 2048, 1024, 8, 4, 8, 8
NEG = -30000.0
DBG = False
FENCE = False


def _drain(e):
    return e.drain()


def _hard(e):
    return e.drain()


def wmax(*toks):
    w = {}
    for t in toks:
        if t is None:
            continue
        if isinstance(t, dict):
            for s, v in t.items():
                if v and w.get(s, 0) < v:
                    w[s] = v
        else:
            s, v = t
            if v and w.get(s, 0) < v:
                w[s] = v
    return w


class Sched:
    def __init__(self):
        self.ops = []
        self.cnt = {}

    def op(self, eng, fn, waits=None, inc=None, amt=1):
        w = {s: v for s, v in (waits or {}).items() if v}
        self.ops.append((eng, fn, w, inc, amt))
        if inc is not None:
            self.cnt[inc] = self.cnt.get(inc, 0) + amt
            return (inc, self.cnt[inc])
        return None

    def tok(self, sem):
        return (sem, self.cnt.get(sem, 0))


def build_program():
    nc = bass.Bass("TRN2", target_bir_lowering=False)

    xT = nc.dram_tensor("xT", [D, T], F32, kind="ExternalInput").ap()
    wT = {nm: nc.dram_tensor(nm, [D, D], F32, kind="ExternalInput").ap()
          for nm in ["wqT", "wkT", "wvT", "woT", "wfT"]}
    bdr = {nm: nc.dram_tensor(nm, [128, 8], F32, kind="ExternalInput").ap()
           for nm in ["bq", "bk", "bv", "bo", "bf"]}
    ident_d = nc.dram_tensor("ident", [128, 128], F32, kind="ExternalInput").ap()
    mask_d = nc.dram_tensor("maskadd", [128, 2048], BF16, kind="ExternalInput").ap()
    outT = nc.dram_tensor("outT", [D, T], F32, kind="ExternalOutput").ap()
    qH_sc = nc.dram_tensor("qH_sc", [D, T], BF16).ap()
    qL_sc = nc.dram_tensor("qL_sc", [D, T], BF16).ap()
    kH_sc = nc.dram_tensor("kH_sc", [D, T], BF16).ap()
    kL_sc = nc.dram_tensor("kL_sc", [D, T], BF16).ap()
    v_sc = nc.dram_tensor("v_sc", [T, D], F32).ap()
    att_sc = nc.dram_tensor("att_sc", [D, T], F32).ap()
    cc_in_t = nc.dram_tensor("cc_in", [1, 64], F32)
    cc_out_t = nc.dram_tensor("cc_out", [1, 64], F32)

    AR = nc.alloc_sbuf_tensor("AR", [128, 19968], F32)
    STT = nc.alloc_sbuf_tensor("STT", [128, 5184], F32)
    masks = nc.alloc_sbuf_tensor("masks", [128, 2048], BF16)
    xnspl = nc.alloc_sbuf_tensor("xnspl", [128, 8192], BF16)
    xnspl2 = nc.alloc_sbuf_tensor("xnspl2", [128, 8192], BF16)
    wspl = nc.alloc_sbuf_tensor("wspl", [128, 16384], BF16)
    kqb = nc.alloc_sbuf_tensor("kqb", [128, 10240], BF16)
    evb = nc.alloc_sbuf_tensor("evb", [128, 2048], BF16)
    ident = nc.alloc_sbuf_tensor("ident_s", [128, 128], F32)
    ones128 = nc.alloc_sbuf_tensor("ones128", [128, 1], F32)
    ones1 = nc.alloc_sbuf_tensor("ones1", [1, 128], F32)
    bias = {nm: nc.alloc_sbuf_tensor("s" + nm, [128, 8], F32)
            for nm in ["bq", "bk", "bv", "bo", "bf"]}
    scl = nc.alloc_sbuf_tensor("scl", [128, 8], F32)
    smal = nc.alloc_sbuf_tensor("smal", [1, 128], F32)
    ccst = nc.alloc_sbuf_tensor("ccst", [1, 64], F32)
    ccrd = nc.alloc_sbuf_tensor("ccrd", [1, 64], F32)
    qt8 = nc.alloc_sbuf_tensor("qt8", [128, 1024], FP8)
    qtb = nc.alloc_sbuf_tensor("qtb", [128, 1536], BF16)
    qmsk = nc.alloc_sbuf_tensor("qmsk", [128, 512], U8)
    qt8b = nc.alloc_sbuf_tensor("qt8b", [128, 1024], FP8)
    qtbB = nc.alloc_sbuf_tensor("qtbB", [128, 512], BF16)
    qmskb = nc.alloc_sbuf_tensor("qmskb", [128, 512], U8)
    ffb = nc.alloc_sbuf_tensor("ffb", [128, 1024], BF16)

    A = lambda a, b: AR[:, a:b]
    WRING = [A(s * 1024, (s + 1) * 1024) for s in range(4)]
    XSL = [A(4096 + c * 512, 4096 + (c + 1) * 512) for c in range(CH)]
    X2B = [A(8192 + i * 512, 8192 + (i + 1) * 512) for i in range(2)]
    XNT = A(9216, 9728)
    EVQ = [A(9728 + i * 512, 9728 + (i + 1) * 512) for i in range(2)]
    VST = [A(10752 + k * 1024, 10752 + (k + 1) * 1024) for k in range(4)]
    VSTT = AR[:, 10752:14848]
    PTIL = [A(h * 2048, (h + 1) * 2048) for h in range(2)]
    PT = [A(4096 + h * 2048, 4096 + (h + 1) * 2048) for h in range(2)]
    OTS = [A(8192 + p * 512, 8192 + (p + 1) * 512) for p in range(PAIRS)]
    VBUFS = [A(12288 + i * 2048, 12288 + (i + 1) * 2048) for i in range(2)]
    RB = [A(16384 + h * 512, 16384 + (h + 1) * 512) for h in range(2)]
    OSTG = [A(17408 + i * 512, 17408 + (i + 1) * 512) for i in range(2)]
    SUMC = A(18432, 18440)
    RECC = A(18440, 18448)
    SMP = A(18448, 18576)
    EV3 = AR[0:8, 18576:18704]
    RRW = AR[0:1, 18704:19728]
    XRE = [A(4096 + c * 512, 4096 + (c + 1) * 512) for c in range(CH)]
    XN2 = [A(8192 + c * 512, 8192 + (c + 1) * 512) for c in range(CH)]
    QTMP = [A(12288 + i * 512, 12288 + (i + 1) * 512) for i in range(2)]
    FFS = [A(13312 + i * 512, 13312 + (i + 1) * 512) for i in range(4)]
    ATT = [A(15360 + i * 512, 15360 + (i + 1) * 512) for i in range(2)]

    XSH = [xnspl[:, c * 512:(c + 1) * 512] for c in range(CH)]
    XSLO = [xnspl[:, 4096 + c * 512: 4096 + (c + 1) * 512] for c in range(CH)]
    XSH2 = [xnspl2[:, c * 512:(c + 1) * 512] for c in range(CH)]
    XSLO2 = [xnspl2[:, 4096 + c * 512: 4096 + (c + 1) * 512] for c in range(CH)]
    WSH = [wspl[:, c * 1024:(c + 1) * 1024] for c in range(CH)]
    WSLO = [wspl[:, 8192 + c * 1024: 8192 + (c + 1) * 1024] for c in range(CH)]
    KH = [kqb[:, pp * 2048:(pp + 1) * 2048] for pp in range(2)]
    KL = [kqb[:, 4096 + pp * 2048: 4096 + (pp + 1) * 2048] for pp in range(2)]
    QH = [kqb[:, 8192 + pp * 512: 8192 + (pp + 1) * 512] for pp in range(2)]
    QL = [kqb[:, 9216 + pp * 512: 9216 + (pp + 1) * 512] for pp in range(2)]
    EVH = [evb[:, i * 512:(i + 1) * 512] for i in range(2)]
    EVL = [evb[:, 1024 + i * 512: 1024 + (i + 1) * 512] for i in range(2)]
    X8 = XSH
    W8 = WSH

    MROW = STT[0:1, 0:2048]
    YROW = STT[0:1, 2048:4096]
    SCR = lambda a, b: STT[0:1, 4096 + a: 4096 + b]
    AMP = STT[:, 5120:5152]
    ACOL = STT[:, 5152:5153]
    WAMP = STT[:, 5160:5168]
    WCOL = STT[:, 5168:5169]

    pb = [nc.alloc_psum_tensor("pb%d" % i, [128, 512], F32) for i in range(8)]

    sc = Sched()
    bankF = [None] * 8

    def recip_op(out_ap, in_ap, waits=None):
        sc.op("vector", _drain)
        sc.op("vector", lambda e, o=out_ap, i=in_ap: e.reciprocal(o, i), waits=waits)
        return sc.op("vector", lambda e: e.drain(), inc="ve")

    def vop_d(fn, waits=None):
        sc.op("vector", fn, waits=waits)
        return sc.op("vector", _hard, inc="ve")

    sc.op("vector", lambda e: e.memset(ones128[:], 1.0))
    sc.op("vector", lambda e: e.memset(ones1[:], 1.0))
    sc.op("vector", lambda e: e.memset(ccst[:], 0.0))
    sc.op("vector", lambda e: e.memset(smal[:], 0.0))
    t_consts = sc.op("vector", _drain, inc="ve")
    sc.op("sync", lambda e: e.dma_start(out=ident[:], in_=ident_d), inc="ld", amt=16)
    sc.op("sync", lambda e: e.dma_start(out=masks[:], in_=mask_d), inc="ld", amt=16)
    for nm in ["bq", "bk", "bv", "bo", "bf"]:
        sc.op("sync", lambda e, nm=nm: e.dma_start(
            out=bias[nm][:, :], in_=bdr[nm]), inc="ld", amt=16)
    t_const_ld = sc.tok("ld")

    wringF = [None] * 4
    g_chunk = [0]

    def ring_dma(wnm, c, extra=None):
        slot = g_chunk[0] % 4
        tk = sc.op("sync", lambda e, wnm=wnm, c=c, slot=slot: e.dma_start(
            out=WRING[slot][:], in_=wT[wnm][c * 128:(c + 1) * 128, :]),
            waits=wmax(wringF[slot], extra), inc="lw", amt=16)
        g_chunk[0] += 1
        return slot, tk

    evqF = [None, None]
    evbF = [None, None]
    wspF = [None] * CH
    xslF = [None] * CH
    xsetF = [[None] * CH, [None] * CH]
    t_xn_done = [None]
    XSHD = [XSH, XSH2]
    XSLOD = [XSLO, XSLO2]
    st = {}

    def stats_a(tc):
        c0, c1 = tc * 512, (tc + 1) * 512
        xs_tok = [None] * CH
        for c in range(CH):
            xs_tok[c] = sc.op("sync", lambda e, c=c, c0=c0, c1=c1: e.dma_start(
                out=XSL[c][:], in_=xT[c * 128:(c + 1) * 128, c0:c1]),
                waits=wmax(xslF[c]), inc="ld", amt=16)
        st[tc] = {"xs": xs_tok}

    def stats_b(tc):
        c0, c1 = tc * 512, (tc + 1) * 512
        xs_tok = st[tc]["xs"]
        sq_tok = [None] * CH
        sq_mm = [None] * CH
        for c in range(CH):
            sc.op("scalar", lambda e, c=c: e.activation(
                X2B[c % 2][:], XSL[c][:], Act.Square),
                waits=wmax(xs_tok[c], t_xn_done[0], sq_mm[c - 2] if c >= 2 else None))
            sq_tok[c] = sc.op("scalar", _drain, inc="se")
            sc.op("tensor", lambda e, c=c, st_=(c == 0), sp_=(c == CH - 1): e.matmul(
                pb[6][0:1, :], ones128[:], XSL[c][:], start=st_, stop=sp_),
                waits=wmax(xs_tok[c], t_consts, bankF[6] if c == 0 else None), inc="pe")
            sq_mm[c] = sc.op("tensor", lambda e, c=c, st_=(c == 0), sp_=(c == CH - 1): e.matmul(
                pb[7][0:1, :], ones128[:], X2B[c % 2][:], start=st_, stop=sp_),
                waits=wmax(sq_tok[c], bankF[7] if c == 0 else None), inc="pe")
        t_sums = sq_mm[CH - 1]
        mrow, yrow = MROW[0:1, c0:c1], YROW[0:1, c0:c1]
        e2, mmv = SCR(0, 512), SCR(512, 1024)
        sc.op("vector", lambda e, mrow=mrow: e.tensor_scalar(
            mrow, pb[6][0:1, :], 1.0 / 1024.0, None, op0=Alu.mult),
            waits=wmax(t_sums))
        sc.op("vector", _drain, inc="ve")
        tb = sc.op("vector", lambda e, e2=e2: e.tensor_scalar(
            e2, pb[7][0:1, :], 1.0 / 1024.0, None, op0=Alu.mult), inc="ve")
        bankF[6] = bankF[7] = tb
        sc.op("vector", lambda e, mmv=mmv, mrow=mrow: e.tensor_tensor(
            mmv, mrow, mrow, op=Alu.mult), inc="ve")
        sc.op("vector", lambda e, e2=e2, mmv=mmv: e.tensor_tensor(
            e2, e2, mmv, op=Alu.subtract), inc="ve")
        sc.op("vector", lambda e, e2=e2: e.tensor_scalar(
            e2, e2, 1e-5, None, op0=Alu.add), inc="ve")
        t_var = recip_op(mmv, e2)
        sc.op("scalar", lambda e, yrow=yrow, mmv=mmv: e.activation(
            yrow, mmv, Act.Sqrt), waits=wmax(t_var))
        st[tc]["t_y"] = sc.op("scalar", _hard, inc="se")
        st[tc]["sq_last"] = sq_tok[CH - 1]

    def stats_c(tc):
        c0, c1 = tc * 512, (tc + 1) * 512
        s = tc % 2
        mrow, yrow = MROW[0:1, c0:c1], YROW[0:1, c0:c1]
        sc.op("tensor", lambda e, mrow=mrow: e.matmul(
            pb[4][:, :], ones1[:], mrow, start=True, stop=True),
            waits=wmax(st[tc]["t_y"], t_const_ld, bankF[4]), inc="pe")
        t_yb = sc.op("tensor", lambda e, yrow=yrow: e.matmul(
            pb[5][:, :], ones1[:], yrow, start=True, stop=True),
            waits=wmax(bankF[5]), inc="pe")
        xnl_tok = [None] * CH
        prev_hi = None
        for c in range(CH):
            sc.op("vector", lambda e, c=c: e.tensor_tensor(
                X2B[c % 2][:], XSL[c][:], pb[4][:, :], op=Alu.subtract),
                waits=wmax(t_yb, st[tc]["sq_last"]), inc="ve")
            t_xn = sc.op("vector", lambda e, c=c: e.tensor_tensor(
                XNT[:], X2B[c % 2][:], pb[5][:, :], op=Alu.mult),
                waits=wmax(prev_hi, xsetF[s][c]), inc="ve")
            sc.op("vector", lambda e, c=c, tc=tc: e.tensor_reduce(
                AMP[:, tc * 8 + c: tc * 8 + c + 1], XNT[:], axis=AX,
                op=Alu.max, apply_absolute_value=True), inc="ve")
            sc.op("scalar", lambda e, c=c, s=s: e.copy(
                XSHD[s][c], XNT[:]), waits=wmax(t_xn))
            prev_hi = sc.op("scalar", _drain, inc="se")
            sc.op("vector", lambda e, c=c, s=s: e.tensor_tensor(
                XSLOD[s][c], XNT[:], XSHD[s][c], op=Alu.subtract),
                waits=wmax(prev_hi))
            xnl_tok[c] = sc.op("vector", _drain, inc="ve")
            xslF[c] = t_xn
        st[tc]["xnl"] = xnl_tok
        t_xn_done[0] = sc.tok("ve")
        bankF[4] = bankF[5] = t_xn_done[0]

    def emit_wsplit(wnm, c, hi_eng="scalar"):
        slot, ctk = ring_dma(wnm, c)
        if hi_eng == "scalar":
            sc.op("scalar", lambda e, c=c, slot=slot: e.copy(
                WSH[c], WRING[slot][:]), waits=wmax(ctk, wspF[c]))
            wh = sc.op("scalar", _drain, inc="se")
        else:
            sc.op("vector", lambda e, c=c, slot=slot: e.tensor_copy(
                WSH[c], WRING[slot][:]), waits=wmax(ctk, wspF[c]))
            wh = sc.op("vector", _drain, inc="ve")
        sc.op("vector", lambda e, c=c, slot=slot: e.tensor_tensor(
            WSLO[c], WRING[slot][:], WSH[c], op=Alu.subtract),
            waits=wmax(wh))
        t = sc.op("vector", _drain, inc="ve")
        wringF[slot] = t
        return t

    pend_wl = [None]

    def emit_proj(tc):
        c0, c1 = tc * 512, (tc + 1) * 512
        s = tc % 2
        xnl = st[tc]["xnl"]
        vst_last = [None]
        weights = [("wqT", "bq"), ("wkT", "bk"), ("wvT", "bv")]
        wl_tok = None
        for wi, (wnm, bnm) in enumerate(weights):
            if wi == 0:
                if tc + 1 < TC:
                    stats_a(tc + 1)
                wl_tok = pend_wl[0] if pend_wl[0] is not None else \
                    [emit_wsplit(wnm, c) for c in range(CH)]
            nxt = None
            for half in range(2):
                stopd = {}
                for c in range(CH):
                    for oc4 in range(4):
                        oc = half * 4 + oc4
                        for ps in range(3):
                            ops_ = [(WSH[c], XSHD[s][c]), (WSH[c], XSLOD[s][c]),
                                    (WSLO[c], XSHD[s][c])][ps]
                            tk = sc.op("tensor", lambda e, b=oc4, oc=oc, ops_=ops_,
                                       st_=(c == 0 and ps == 0),
                                       sp_=(c == CH - 1 and ps == 2): e.matmul(
                                pb[b][:, :], ops_[0][:, oc * 128:(oc + 1) * 128], ops_[1],
                                start=st_, stop=sp_),
                                waits=wmax(wl_tok[c], xnl[c],
                                           bankF[oc4] if (c == 0 and ps == 0) else None)
                                if ps == 0 else None, inc="pe")
                            wspF[c] = tk
                            xsetF[s][c] = tk
                        if c == CH - 1:
                            stopd[oc] = tk
                if half == 1:
                    if wi < 2:
                        nxt = [emit_wsplit(weights[wi + 1][0], c) for c in range(CH)]
                    elif tc + 1 < TC:
                        nxt = [emit_wsplit("wqT", c, hi_eng="vector")
                               for c in range(CH)]
                for oc4 in range(4):
                    oc = half * 4 + oc4
                    b = oc4
                    stop = stopd[oc]
                    if wi < 2:
                        dH, dL = (qH_sc, qL_sc) if wi == 0 else (kH_sc, kL_sc)
                        i = oc % 2
                        sc.op("vector", lambda e, b=b, oc=oc, i=i, bnm=bnm: e.tensor_scalar(
                            EVH[i], pb[b][:, :], bias[bnm][:, oc:oc + 1], None, op0=Alu.add),
                            waits=wmax(stop, evbF[i]), inc="ve")
                        sc.op("vector", lambda e, b=b, oc=oc, i=i, bnm=bnm: e.scalar_tensor_tensor(
                            EVL[i], pb[b][:, :], bias[bnm][:, oc:oc + 1], EVH[i],
                            op0=Alu.add, op1=Alu.subtract))
                        tlo = sc.op("vector", _drain, inc="ve")
                        bankF[b] = tlo
                        sc.op("scalar", lambda e, oc=oc, i=i, dH=dH, c0=c0, c1=c1: e.dma_start(
                            out=dH[oc * 128:(oc + 1) * 128, c0:c1], in_=EVH[i]),
                            waits=wmax(tlo), inc="sd", amt=16)
                        evbF[i] = sc.op("scalar", lambda e, oc=oc, i=i, dL=dL, c0=c0, c1=c1: e.dma_start(
                            out=dL[oc * 128:(oc + 1) * 128, c0:c1], in_=EVL[i]),
                            inc="sd", amt=16)
                    else:
                        ev = EVQ[oc % 2]
                        sc.op("vector", lambda e, b=b, oc=oc, ev=ev: e.tensor_scalar(
                            ev[:], pb[b][:, :], bias["bv"][:, oc:oc + 1], None, op0=Alu.add),
                            waits=wmax(stop, evqF[oc % 2]))
                        tkv = sc.op("vector", _drain, inc="ve")
                        tt = None
                        for kk in range(4):
                            tt = sc.op("tensor", lambda e, b=b, ev=ev, kk=kk: e.transpose(
                                pb[b][:, kk * 128:(kk + 1) * 128],
                                ev[:, kk * 128:(kk + 1) * 128], ident[:]),
                                waits=wmax(tkv) if kk == 0 else None, inc="pe")
                        evqF[oc % 2] = tt
                        sc.op("scalar", lambda e, b=b, oc=oc: e.copy(
                            VSTT.rearrange("p (kk j) -> p kk j", j=1024)[:, :, oc * 128:(oc + 1) * 128],
                            pb[b][:, :]), waits=wmax(tt))
                        vst_last[0] = sc.op("scalar", _drain, inc="se")
                        bankF[b] = vst_last[0]
            if wi == 0 and tc + 1 < TC:
                stats_b(tc + 1)
            if wi == 1 and tc + 1 < TC:
                stats_c(tc + 1)
            if nxt is not None and wi < 2:
                wl_tok = nxt
        pend_wl[0] = nxt if tc + 1 < TC else None
        for kk in range(4):
            sc.op("scalar", lambda e, kk=kk, c0=c0: e.dma_start(
                out=v_sc[c0 + kk * 128: c0 + (kk + 1) * 128, :], in_=VST[kk][:]),
                waits=wmax(vst_last[0]), inc="sd", amt=16)

    stats_a(0)
    stats_b(0)
    stats_c(0)
    for tc in range(TC):
        emit_proj(tc)

    t_p1_pe, t_p1_ve = sc.tok("pe"), sc.tok("ve")
    t_p1_se, t_p1_sd = sc.tok("se"), sc.tok("sd")

    sc.op("vector", lambda e: e.tensor_reduce(
        ACOL, AMP[:, :], axis=AX, op=Alu.max))
    t_r1 = sc.op("vector", _hard, inc="ve")
    t_tr = sc.op("tensor", lambda e: e.transpose(
        pb[2][0:1, 0:128], ACOL, ident[:]), waits=wmax(t_r1, bankF[2]), inc="pe")
    sc.op("vector", lambda e: e.tensor_reduce(
        ccst[0:1, 0:1], pb[2][0:1, 0:128], axis=AX, op=Alu.max),
        waits=wmax(t_tr))
    t_r2 = sc.op("vector", _hard, inc="ve")
    bankF[2] = t_r2
    t_ccw = sc.op("gpsimd", lambda e: e.dma_start(out=cc_in_t.ap(), in_=ccst[:]),
                  waits=wmax(t_r2), inc="gd", amt=16)
    t_cc = sc.op("gpsimd", lambda e: e.collective_compute(
        "AllReduce", Alu.max, replica_groups=[list(range(NCORE))],
        ins=[cc_in_t.ap().opt()], outs=[cc_out_t.ap().opt()]),
        waits=wmax(t_ccw), inc="cc")
    t_ccr = sc.op("gpsimd", lambda e: e.dma_start(out=ccrd[:], in_=cc_out_t.ap()),
                  waits=wmax(t_cc), inc="gd", amt=16)

    xspF = list(xsetF[0])
    kF = [t_p1_pe, t_p1_pe]
    qF = [t_p1_pe, t_p1_pe]
    vF = [t_p1_pe, t_p1_pe]
    ptF = [t_p1_pe, t_p1_pe]
    otsF = t_p1_pe
    smpF = [t_p1_ve]
    rbF = [t_p1_ve, t_p1_ve]
    oproj_end = t_p1_pe
    ostgF = [None, None]

    RECCF = [None]
    pend = [None]

    def emit_sums_flush():
        if pend[0] is None:
            return
        p_, t_rc_, pvev_, nkt_ = pend[0]
        pend[0] = None
        t_tr2 = None
        for j_ in range(4):
            t_tr2 = sc.op("tensor", lambda e, j_=j_: e.transpose(
                pb[4][0:1, j_ * 128:(j_ + 1) * 128], RECC[:, j_:j_ + 1], ident[:]),
                waits=wmax(t_rc_, bankF[4]) if j_ == 0 else None, inc="pe")
        for j_ in range(4):
            t_tr2 = sc.op("tensor", lambda e, j_=j_: e.transpose(
                pb[5][0:1, j_ * 128:(j_ + 1) * 128], RECC[:, 4 + j_:5 + j_], ident[:]),
                waits=wmax(bankF[5]) if j_ == 0 else None, inc="pe")
        RECCF[0] = t_tr2
        sc.op("scalar", lambda e: e.copy(RRW[0:1, 0:512], pb[4][0:1, 0:512]),
              waits=wmax(t_tr2))
        sc.op("scalar", lambda e: e.copy(RRW[0:1, 512:1024], pb[5][0:1, 0:512]))
        t_e3 = sc.op("scalar", _hard, inc="se")
        t_b0 = sc.op("tensor", lambda e: e.matmul(
            pb[4][:, :], ones1[:], RRW[0:1, 0:512], start=True, stop=True),
            waits=wmax(t_e3), inc="pe")
        t_b1 = sc.op("tensor", lambda e: e.matmul(
            pb[5][:, :], ones1[:], RRW[0:1, 512:1024], start=True, stop=True),
            inc="pe")
        t_rb0 = sc.op("scalar", lambda e: e.copy(RB[0][:], pb[4][:, :]),
                      waits=wmax(t_b0, rbF[0]), inc="se")
        sc.op("scalar", lambda e: e.copy(RB[1][:], pb[5][:, :]),
              waits=wmax(t_b1, rbF[1]))
        t_rb1 = sc.op("scalar", _drain, inc="se")
        bankF[4] = t_rb0
        bankF[5] = t_rb1
        t_n0 = sc.op("vector", lambda e, p_=p_: e.tensor_tensor(
            OTS[p_][0:64, :], OTS[p_][0:64, :], RB[0][0:64, :], op=Alu.mult),
            waits=wmax(t_rb1, pvev_), inc="ve")
        sc.op("vector", lambda e, p_=p_: e.tensor_tensor(
            OTS[p_][64:128, :], OTS[p_][64:128, :], RB[1][64:128, :], op=Alu.mult))
        t_n1 = sc.op("vector", _drain, inc="ve")
        rbF[0] = t_n0
        rbF[1] = t_n1
        sc.op("scalar", lambda e, p_=p_: e.copy(
            XSH[p_], OTS[p_][:, :]), waits=wmax(t_n1, xspF[p_]))
        oh = sc.op("scalar", _drain, inc="se")
        sc.op("vector", lambda e, p_=p_: e.tensor_tensor(
            XSLO[p_], OTS[p_][:, :], XSH[p_], op=Alu.subtract),
            waits=wmax(oh))
        ol_tok[p_] = sc.op("vector", _drain, inc="ve")

    ol_tok = [None] * CH
    wo_tok = [None] * CH
    wam = {}
    for qt in range(TC):
        q0, q1 = qt * 512, (qt + 1) * 512
        nkt = qt + 1
        nkc = nkt * 4
        for p in range(PAIRS):
            pp = p % 2
            tq1 = sc.op("sync", lambda e, p=p, q0=q0, q1=q1, pp=pp: e.dma_start(
                out=QH[pp][:, :], in_=qH_sc[p * 128:(p + 1) * 128, q0:q1]),
                waits=wmax(t_p1_sd, qF[pp]), inc="lq", amt=16)
            tq2 = sc.op("sync", lambda e, p=p, q0=q0, q1=q1, pp=pp: e.dma_start(
                out=QL[pp][:, :], in_=qL_sc[p * 128:(p + 1) * 128, q0:q1]),
                inc="lq", amt=16)
            tk1 = sc.op("sync", lambda e, p=p, pp=pp, nkt=nkt: e.dma_start(
                out=KH[pp][:, 0:nkt * 512], in_=kH_sc[p * 128:(p + 1) * 128, 0:nkt * 512]),
                waits=wmax(kF[pp]), inc="lq", amt=16)
            tk2 = sc.op("sync", lambda e, p=p, pp=pp, nkt=nkt: e.dma_start(
                out=KL[pp][:, 0:nkt * 512], in_=kL_sc[p * 128:(p + 1) * 128, 0:nkt * 512]),
                inc="lq", amt=16)
            tqk = wmax(tq1, tq2, tk1, tk2)
            t_v = None
            for kc in range(nkc):
                t_v = sc.op("sync", lambda e, p=p, kc=kc, pp=pp: e.dma_start(
                    out=VBUFS[pp][:, kc * 128:(kc + 1) * 128],
                    in_=v_sc[kc * 128:(kc + 1) * 128, p * 128:(p + 1) * 128]),
                    waits=wmax(t_p1_sd, vF[pp]), inc="sv", amt=16)
            exp_tok = {}
            pv_ev = {}
            lastT = [None]

            def emit_S_exp(qc):
                order = [(h, kt) for kt in range(nkt) for h in range(2)]
                s_tok = {}
                for idx, (h, kt) in enumerate(order):
                    b = 2 * ((qc * nkt + kt) % 2) + h
                    t_s = None
                    for ps in range(3):
                        qop = [QH, QH, QL][ps][pp]
                        kop = [KH, KL, KH][ps][pp]
                        t_s = sc.op("tensor",
                                    lambda e, h=h, qc=qc, kt=kt, b=b, ps=ps,
                                    qop=qop, kop=kop: e.matmul(
                            pb[b][:, :],
                            qop[h * 64:(h + 1) * 64, qc * 128:(qc + 1) * 128],
                            kop[h * 64:(h + 1) * 64, kt * 512:(kt + 1) * 512],
                            start=(ps == 0), stop=(ps == 2),
                            tile_position=(h * 64, 0)),
                            waits=wmax(tqk, bankF[b], oproj_end)
                            if ps == 0 else None, inc="pe")
                    s_tok[(h, kt)] = t_s
                    if idx >= 2:
                        emit_exp(qc, *order[idx - 2], s_tok)
                for (h, kt) in order[-2:]:
                    emit_exp(qc, h, kt, s_tok)

            def emit_exp(qc, h, kt, s_tok):
                b = 2 * ((qc * nkt + kt) % 2) + h
                wexp = wmax(s_tok[(h, kt)], smpF[0], lastT[0])
                if kt == qt:
                    sc.op("vector", lambda e, b=b, qc=qc: e.tensor_tensor(
                        pb[b][:, :], pb[b][:, :],
                        masks[:, qc * 512:(qc + 1) * 512], op=Alu.add),
                        waits=wmax(s_tok[(h, kt)]))
                    t_m = sc.op("vector", _drain, inc="ve")
                    wexp = wmax(t_m, smpF[0], lastT[0])
                col = h * 64 + qc * 16 + kt
                sc.op("scalar", lambda e, h=h, kt=kt, b=b, col=col: e.activation(
                    PTIL[h][:, kt * 512:(kt + 1) * 512], pb[b][:, :], Act.Exp,
                    bias=0.0, scale=1.0 / 32.0,
                    accum_out=SMP[:, col:col + 1]), waits=wexp)
                te = sc.op("scalar", _drain, inc="se")
                bankF[b] = te
                exp_tok[(qc, h, kt)] = te

            def emit_tpv(qc):
                tpF = {}
                tt = None
                for h in range(2):
                    for g in range(nkt):
                        tbk = 4 + ((h * nkt + g) % 2)
                        for i in range(4):
                            kc = g * 4 + i
                            tt = sc.op("tensor", lambda e, h=h, kc=kc, i=i, tbk=tbk: e.transpose(
                                pb[tbk][:, i * 128:(i + 1) * 128],
                                PTIL[h][:, kc * 128:(kc + 1) * 128], ident[:]),
                                waits=wmax(exp_tok[(qc, h, g)], bankF[tbk],
                                           ptF[h] if g == 0 else None)
                                if i == 0 else None, inc="pe")
                        eng, sem = "vector", "ve"
                        sc.op(eng, lambda e, h=h, g=g, tbk=tbk: e.tensor_copy(
                            PT[h][:, g * 512:(g + 1) * 512], pb[tbk][:, :]),
                            waits=wmax(tt))
                        tev = sc.op(eng, _drain, inc=sem)
                        bankF[tbk] = tev
                        tpF[h] = tev
                lastT[0] = tt
                tpv = {}
                for kc in range(nkc):
                    tpv[0] = sc.op("tensor", lambda e, kc=kc, nkc=nkc, pp=pp: e.matmul(
                        pb[6][0:64, 0:128], VBUFS[pp][:, kc * 128: kc * 128 + 64],
                        PT[0][:, kc * 128:(kc + 1) * 128],
                        start=(kc == 0), stop=(kc == nkc - 1), tile_position=(0, 0)),
                        waits=wmax(t_v, tpF[0], bankF[6]) if kc == 0 else None,
                        inc="pe")
                    tpv[1] = sc.op("tensor", lambda e, kc=kc, nkc=nkc, pp=pp: e.matmul(
                        pb[7][64:128, 0:128], VBUFS[pp][:, kc * 128 + 64: kc * 128 + 128],
                        PT[1][:, kc * 128:(kc + 1) * 128],
                        start=(kc == 0), stop=(kc == nkc - 1), tile_position=(0, 64)),
                        waits=wmax(t_v, tpF[1], bankF[7]) if kc == 0 else None,
                        inc="pe")
                t_e0 = sc.op("vector", lambda e, p=p, qc=qc: e.tensor_copy(
                    OTS[p][0:64, qc * 128:(qc + 1) * 128], pb[6][0:64, 0:128]),
                    waits=wmax(tpv[0], otsF if qc == 0 else None,
                               t_p1_sd if qc == 0 else None), inc="ve")
                sc.op("vector", lambda e, p=p, qc=qc: e.tensor_copy(
                    OTS[p][64:128, qc * 128:(qc + 1) * 128], pb[7][64:128, 0:128]),
                    waits=wmax(tpv[1]))
                t_e1 = sc.op("vector", _drain, inc="ve")
                bankF[6] = t_e0
                bankF[7] = t_e1
                pv_ev[qc] = t_e1

            emit_S_exp(0)
            emit_sums_flush()
            for qc in range(1, 4):
                emit_tpv(qc - 1)
                emit_S_exp(qc)
            t_slast = sc.tok("pe")
            emit_tpv(3)
            t_rd1 = None
            for h in range(2):
                for qc in range(4):
                    cb = h * 64 + qc * 16
                    t_rd1 = sc.op("vector", lambda e, h=h, qc=qc, cb=cb, nkt=nkt: e.tensor_reduce(
                        SUMC[:, h * 4 + qc: h * 4 + qc + 1], SMP[:, cb:cb + nkt],
                        axis=AX, op=Alu.add),
                        waits=wmax(exp_tok[(qc, h, qt)]), inc="ve")
            smpF[0] = t_rd1
            t_rc = recip_op(RECC[:, 0:8], SUMC[:, 0:8], waits=wmax(t_rd1, RECCF[0]))
            pend[0] = (p, t_rc, pv_ev[3], nkt)
            end_pe = sc.tok("pe")
            kF[pp] = t_slast
            qF[pp] = t_slast
            vF[pp] = end_pe
            ptF = [sc.tok("pe"), sc.tok("pe")]
        t_pairs_pe = sc.tok("pe")
        if qt == TC - 1:
            t_pt_se = sc.tok("se")
            t_wred = None
            for c_ in range(CH):
                slot_, ctk_ = ring_dma("wfT", c_, extra=wmax(t_pairs_pe, t_pt_se))
                sc.op("vector", lambda e, c_=c_, slot_=slot_: e.tensor_reduce(
                    WAMP[:, c_:c_ + 1], WRING[slot_][:], axis=AX, op=Alu.max,
                    apply_absolute_value=True), waits=wmax(ctk_))
                t_wred = sc.op("vector", _hard, inc="ve")
                wringF[slot_] = t_wred
            sc.op("vector", lambda e: e.tensor_reduce(
                WCOL, WAMP[:, 0:8], axis=AX, op=Alu.max))
            wam["t_wc"] = sc.op("vector", _hard, inc="ve")
        if qt == 0:
            for s in range(4):
                wringF[s] = wmax(wringF[s], t_pairs_pe)
            for c in range(CH):
                slot, ctk = ring_dma("woT", c)
                sc.op("scalar", lambda e, c=c, slot=slot: e.copy(
                    WSH[c], WRING[slot][:]), waits=wmax(ctk, wspF[c]))
                wh = sc.op("scalar", _drain, inc="se")
                sc.op("vector", lambda e, c=c, slot=slot: e.tensor_tensor(
                    WSLO[c], WRING[slot][:], WSH[c], op=Alu.subtract),
                    waits=wmax(wh))
                wo_tok[c] = sc.op("vector", _drain, inc="ve")
                wringF[slot] = wo_tok[c]
        wl_tok = wo_tok
        stop_tok = [None] * CH

        def oproj_mm(c, oc):
            tkk = None
            for ps in range(3):
                ops_ = [(WSH[c], XSH[c]), (WSH[c], XSLO[c]), (WSLO[c], XSH[c])][ps]
                tkk = sc.op("tensor", lambda e, c=c, oc=oc, ops_=ops_,
                            st_=(c == 0 and ps == 0),
                            sp_=(c == CH - 1 and ps == 2): e.matmul(
                    pb[oc][:, :], ops_[0][:, oc * 128:(oc + 1) * 128], ops_[1],
                    start=st_, stop=sp_),
                    waits=wmax(wl_tok[c], ol_tok[c],
                               bankF[oc] if (c == 0 and ps == 0) else None)
                    if ps == 0 else None, inc="pe")
            if c == CH - 1:
                stop_tok[oc] = tkk
            wspF[c] = tkk
            xspF[c] = tkk

        def oproj_evac(oc):
            i = oc % 2
            sc.op("vector", lambda e, oc=oc, i=i: e.tensor_scalar(
                OSTG[i][:], pb[oc][:, :], bias["bo"][:, oc:oc + 1], None, op0=Alu.add),
                waits=wmax(stop_tok[oc], ostgF[i]))
            tev = sc.op("vector", _drain, inc="ve")
            bankF[oc] = tev
            ostgF[i] = sc.op("scalar", lambda e, oc=oc, i=i, q0=q0, q1=q1: e.dma_start(
                out=att_sc[oc * 128:(oc + 1) * 128, q0:q1], in_=OSTG[i][:]),
                waits=wmax(tev), inc="sg", amt=16)

        for c in range(CH - 1):
            for oc in range(4):
                oproj_mm(c, oc)
        emit_sums_flush()
        for oc in range(4):
            oproj_mm(CH - 1, oc)
        for oc in range(4):
            oproj_evac(oc)
        for c in range(CH):
            for oc in range(4, CH):
                oproj_mm(c, oc)
        for oc in range(4, CH):
            oproj_evac(oc)
        oproj_end = sc.tok("pe")
        otsF = oproj_end

    t_p3_pe, t_p3_ve, t_p3_se = sc.tok("pe"), sc.tok("ve"), sc.tok("se")
    t_p3_sd = sc.tok("sd")
    t_p3_sg = sc.tok("sg")

    t_wtr = sc.op("tensor", lambda e: e.transpose(
        pb[2][0:1, 0:128], WCOL, ident[:]), waits=wmax(wam["t_wc"], bankF[2]), inc="pe")
    sc.op("vector", lambda e: e.tensor_reduce(
        smal[0:1, 2:3], pb[2][0:1, 0:128], axis=AX, op=Alu.max),
        waits=wmax(t_wtr))
    t_wamax = sc.op("vector", _hard, inc="ve")
    bankF[2] = t_wamax
    vop_d(lambda e: e.tensor_scalar(
        smal[0:1, 4:5], ccrd[0:1, 0:1], 1e-12, None, op0=Alu.max),
        waits=wmax(t_ccr, t_wamax))
    recip_op(smal[0:1, 6:7], smal[0:1, 4:5])
    vop_d(lambda e: e.tensor_scalar(
        smal[0:1, 6:7], smal[0:1, 6:7], 448.0, 448.0, op0=Alu.mult, op1=Alu.min))
    recip_op(smal[0:1, 7:8], smal[0:1, 6:7])
    vop_d(lambda e: e.tensor_scalar(
        smal[0:1, 9:10], smal[0:1, 2:3], 1e-12, None, op0=Alu.max))
    recip_op(smal[0:1, 10:11], smal[0:1, 9:10])
    vop_d(lambda e: e.tensor_scalar(
        smal[0:1, 10:11], smal[0:1, 10:11], 448.0, 448.0, op0=Alu.mult, op1=Alu.min))
    recip_op(smal[0:1, 11:12], smal[0:1, 10:11])
    t_cmul = vop_d(lambda e: e.tensor_tensor(
        smal[0:1, 12:13], smal[0:1, 7:8], smal[0:1, 11:12], op=Alu.mult))
    bc_done = None
    prev_bc = None
    for src, dst in [(12, 0), (6, 1), (10, 2)]:
        tbc = sc.op("tensor", lambda e, src=src: e.matmul(
            pb[3][:, 0:1], ones1[:], smal[0:1, src:src + 1], start=True, stop=True),
            waits=wmax(t_cmul, bankF[3], prev_bc), inc="pe")
        sc.op("vector", lambda e, dst=dst: e.tensor_copy(
            scl[:, dst:dst + 1], pb[3][:, 0:1]), waits=wmax(tbc))
        bc_done = sc.op("vector", _hard, inc="ve")
        prev_bc = bc_done
        bankF[3] = bc_done

    QSET = [
        {"t0": QTMP[0], "t1": QTMP[1], "q8": qt8,
         "ob": qtb[:, 0:512], "mk": qmsk[:, 0:512]},
        {"t0": STT[:, 4096:4608], "t1": STT[:, 4608:5120], "q8": qt8b,
         "ob": qtbB[:, 0:512], "mk": qmskb[:, 0:512]},
    ]
    lastq = [{"se": None, "ve": None}, {"se": None, "ve": None}]

    def quant_q(src_ap, dst_ap, scale_col, qs, extra=None):
        B, L = QSET[qs], lastq[qs]
        t_in = sc.op("vector", lambda e, B=B, src_ap=src_ap, scale_col=scale_col:
                     e.tensor_scalar(B["t0"][:], src_ap, scl[:, scale_col:scale_col + 1],
                                     448.0, op0=Alu.mult, op1=Alu.min),
                     waits=wmax(bc_done, L["ve"], L["se"], extra), inc="ve")
        sc.op("vector", lambda e, B=B: e.tensor_scalar(
            B["t0"][:], B["t0"][:], -448.0, None, op0=Alu.max), inc="ve")
        sc.op("vector", lambda e, B=B: e.tensor_copy(B["q8"][:, 0:512], B["t0"][:]))
        t_8a = sc.op("vector", _drain, inc="ve")
        sc.op("vector", lambda e, B=B: e.tensor_scalar(
            B["t1"][:], B["t0"][:], 0.5, None, op0=Alu.mult), inc="ve")
        sc.op("vector", lambda e, B=B: e.tensor_copy(B["q8"][:, 512:1024], B["t1"][:]))
        t_8b = sc.op("vector", _drain, inc="ve")
        sc.op("scalar", lambda e, B=B, dst_ap=dst_ap: e.copy(dst_ap, B["q8"][:, 0:512]),
              waits=wmax(t_8a, L["se"]))
        t_da = sc.op("scalar", _drain, inc="se")
        sc.op("scalar", lambda e, B=B: e.mul(B["ob"], B["q8"][:, 512:1024], 2.0),
              waits=wmax(t_8b))
        sc.op("scalar", lambda e, B=B: e.activation(B["t1"][:], B["t0"][:], Act.Abs))
        t_ob = sc.op("scalar", _drain, inc="se")
        sc.op("vector", lambda e, B=B: e.tensor_scalar(
            B["mk"], B["t1"][:], 240.0, None, op0=Alu.is_ge),
            waits=wmax(t_ob), inc="ve")
        sc.op("vector", lambda e, B=B, dst_ap=dst_ap: e.copy_predicated(
            dst_ap, B["mk"], B["ob"]), waits=wmax(t_da))
        t_cp = sc.op("vector", _drain, inc="ve")
        L["se"] = t_ob
        L["ve"] = t_cp
        return t_cp, t_in

    w8_tok = None
    for c in range(CH):
        slot, ctk = ring_dma("wfT", c)
        rdF = None
        for half in range(2):
            w8_tok, t_rd = quant_q(WRING[slot][:, half * 512:(half + 1) * 512],
                                   W8[c][:, half * 512:(half + 1) * 512], 2, half,
                                   extra=wmax(ctk, wspF[c] if half == 0 else None))
            rdF = wmax(rdF, t_rd)
        wringF[slot] = rdF
    t_w8nop = sc.op("tensor", lambda e: e.nop(),
                    waits=wmax(w8_tok, lastq[0]["ve"], lastq[1]["ve"]), inc="pe")

    xreF = [None] * CH
    xn2F = [None] * CH
    x8F = [None] * CH
    attF = [None, None]
    ffbF = [None, None]
    ffsF = [None] * 4
    for tc in range(TC):
        c0, c1 = tc * 512, (tc + 1) * 512
        xr_tok = [None] * CH
        for c in range(CH):
            xr_tok[c] = sc.op("sync", lambda e, c=c, c0=c0, c1=c1: e.dma_start(
                out=XRE[c][:], in_=xT[c * 128:(c + 1) * 128, c0:c1]),
                waits=wmax(xreF[c], t_p3_pe if tc == 0 else None), inc="ld", amt=16)
        mrow, yrow = MROW[0:1, c0:c1], YROW[0:1, c0:c1]
        sc.op("tensor", lambda e, mrow=mrow: e.matmul(
            pb[4][:, :], ones1[:], mrow, start=True, stop=True),
            waits=wmax(bankF[4], t_w8nop), inc="pe")
        t_yb = sc.op("tensor", lambda e, yrow=yrow: e.matmul(
            pb[5][:, :], ones1[:], yrow, start=True, stop=True),
            waits=wmax(bankF[5]), inc="pe")
        xn_t = [None] * CH
        for c in range(CH):
            sc.op("vector", lambda e, c=c: e.tensor_tensor(
                XN2[c][:], XRE[c][:], pb[4][:, :], op=Alu.subtract),
                waits=wmax(t_yb, xr_tok[c], xn2F[c], lastq[0]["ve"], lastq[1]["ve"],
                           t_p3_ve if tc == 0 and c == 0 else None), inc="ve")
            xn_t[c] = sc.op("vector", lambda e, c=c: e.tensor_tensor(
                XN2[c][:], XN2[c][:], pb[5][:, :], op=Alu.mult), inc="ve")
        bankF[4] = sc.tok("ve")
        bankF[5] = sc.tok("ve")
        x8_t = [None] * CH
        for c in range(CH):
            x8_t[c], _ = quant_q(XN2[c][:], X8[c][:], 1, c % 2,
                                 extra=wmax(xn_t[c], x8F[c],
                                            xspF[c] if tc == 0 else None))
        stop_tok = [None] * CH
        for c in range(CH):
            for oc in range(CH):
                tkk = sc.op("tensor", lambda e, c=c, oc=oc: e.matmul(
                    pb[oc][:, :], W8[c][:, oc * 128:(oc + 1) * 128], X8[c][:],
                    start=(c == 0), stop=(c == CH - 1)),
                    waits=wmax(x8_t[c], bankF[oc] if c == 0 else None,
                               t_p3_pe if tc == 0 and c == 0 else None), inc="pe")
                if c == CH - 1:
                    stop_tok[oc] = tkk
        for c in range(CH):
            x8F[c] = stop_tok[CH - 1]
            xn2F[c] = xn_t[c]
        for oc in range(CH):
            fb = oc % 2
            ta = sc.op("scalar", lambda e, oc=oc, fb=fb, c0=c0, c1=c1: e.dma_start(
                out=ATT[fb][:], in_=att_sc[oc * 128:(oc + 1) * 128, c0:c1]),
                waits=wmax(attF[fb], t_p3_sg), inc="sg", amt=16)
            t_fe = sc.op("vector", lambda e, oc=oc, fb=fb: e.tensor_scalar(
                ffb[:, fb * 512:(fb + 1) * 512], pb[oc][:, :],
                scl[:, 0:1], bias["bf"][:, oc:oc + 1], op0=Alu.mult, op1=Alu.add),
                waits=wmax(stop_tok[oc], ffbF[fb]), inc="ve")
            bankF[oc] = t_fe
            s0, s1 = fb * 2, fb * 2 + 1
            t_a1 = sc.op("vector", lambda e, oc=oc, fb=fb, s0=s0: e.tensor_tensor(
                FFS[s0][:], XRE[oc][:], ATT[fb][:], op=Alu.add),
                waits=wmax(xr_tok[oc], ta, ffsF[s0]), inc="ve")
            attF[fb] = t_a1
            sc.op("vector", lambda e, s0=s0, s1=s1, fb=fb: e.tensor_tensor(
                FFS[s1][:], FFS[s0][:], ffb[:, fb * 512:(fb + 1) * 512],
                op=Alu.add), waits=wmax(t_fe, ffsF[s1]))
            t_a2 = sc.op("vector", _drain, inc="ve")
            ffbF[fb] = t_a2
            ffsF[s0] = t_a2
            xreF[oc] = t_a2
            t_out = sc.op("scalar", lambda e, oc=oc, c0=c0, c1=c1, s1=s1: e.dma_start(
                out=outT[oc * 128:(oc + 1) * 128, c0:c1], in_=FFS[s1][:]),
                waits=wmax(t_a2), inc="so", amt=16)
            ffsF[s1] = t_out

    if DBG:
        dqh = nc.dram_tensor("dqh", [D, T], BF16, kind="ExternalOutput").ap()
        dql = nc.dram_tensor("dql", [D, T], BF16, kind="ExternalOutput").ap()
        dkh = nc.dram_tensor("dkh", [D, T], BF16, kind="ExternalOutput").ap()
        dkl = nc.dram_tensor("dkl", [D, T], BF16, kind="ExternalOutput").ap()
        dv = nc.dram_tensor("dv", [T, D], F32, kind="ExternalOutput").ap()
        datt = nc.dram_tensor("datt", [D, T], F32, kind="ExternalOutput").ap()
        dsml = nc.dram_tensor("dsml", [1, 128], F32, kind="ExternalOutput").ap()
        dstt = nc.dram_tensor("dstt", [128, 5184], F32, kind="ExternalOutput").ap()
        t_all = wmax(sc.tok("ve"), sc.tok("se"), sc.tok("pe"), sc.tok("sd"), sc.tok("so"))
        sc.op("gpsimd", lambda e: e.dma_start(out=dqh, in_=qH_sc), waits=t_all, inc="so", amt=16)
        sc.op("gpsimd", lambda e: e.dma_start(out=dql, in_=qL_sc), inc="so", amt=16)
        sc.op("gpsimd", lambda e: e.dma_start(out=dkh, in_=kH_sc), inc="so", amt=16)
        sc.op("gpsimd", lambda e: e.dma_start(out=dkl, in_=kL_sc), inc="so", amt=16)
        sc.op("gpsimd", lambda e: e.dma_start(out=dv, in_=v_sc), inc="so", amt=16)
        sc.op("gpsimd", lambda e: e.dma_start(out=datt, in_=att_sc), inc="so", amt=16)
        sc.op("gpsimd", lambda e: e.dma_start(out=dsml, in_=smal[:]), inc="so", amt=16)
        sc.op("gpsimd", lambda e: e.dma_start(out=dstt, in_=STT[:, :]), inc="so", amt=16)
        dx8 = nc.dram_tensor("dx8", [128, 8192], BF16, kind="ExternalOutput").ap()
        dw8 = nc.dram_tensor("dw8", [128, 16384], BF16, kind="ExternalOutput").ap()
        sc.op("gpsimd", lambda e: e.dma_start(out=dx8, in_=xnspl[:, :]), inc="so", amt=16)
        sc.op("gpsimd", lambda e: e.dma_start(out=dw8, in_=wspl[:, :]), inc="so", amt=16)
    sc.op("gpsimd", lambda e: e.nop(), waits=wmax(sc.tok("so")))

    sems = {s: nc.alloc_semaphore(s) for s in
            ["ld", "sd", "gd", "pe", "ve", "se", "cc", "so",
             "lw", "lq", "lr", "sv", "sg"]}
    DMA_SEMS = {"ld", "sd", "gd", "so", "lw", "lq", "lr", "sv", "sg"}
    sc_counts = {}
    with nc.Block() as block:
        def mk(engname):
            def body(eng):
                seen = {}
                prev_dma = {}
                last_ins = [None]
                for (e, fn, waits, inc, amt) in sc.ops:
                    if e != engname:
                        continue
                    if fn is _drain and not FENCE and inc is not None:
                        if last_ins[0] is not None:
                            last_ins[0].then_inc(sems[inc], amt)
                            continue
                    w = dict(waits)
                    if inc in DMA_SEMS:
                        pv = prev_dma.get(inc, 0)
                        if pv and w.get(inc, 0) < pv:
                            w[inc] = pv
                    for s, v in w.items():
                        if seen.get(s, 0) < v:
                            eng.wait_ge(sems[s], v)
                            seen[s] = v
                    ins = fn(eng)
                    if ins is not None:
                        last_ins[0] = ins
                    if inc is not None and ins is not None:
                        ins.then_inc(sems[inc], amt)
                        if inc in DMA_SEMS:
                            prev_dma[inc] = prev_dma.get(inc, 0) + amt
            return body
        block.sync(mk("sync"))
        block.scalar(mk("scalar"))
        block.vector(mk("vector"))
        block.tensor(mk("tensor"))
        block.gpsimd(mk("gpsimd"))
    return nc


_CACHE = {}


def _get_nc():
    if "nc" not in _CACHE:
        _CACHE["nc"] = build_program()
    return _CACHE["nc"]


def kernel(L0_attn_x, L0_attn_Wq_w, L0_attn_Wq_b, L0_attn_Wk_w, L0_attn_Wk_b,
           L0_attn_Wv_w, L0_attn_Wv_b, L0_attn_Wo_w, L0_attn_Wo_b,
           L0_ffn_weight, L0_ffn_bias):
    import ml_dtypes
    x = np.ascontiguousarray(np.asarray(L0_attn_x, dtype=np.float32))
    ws = {
        "wqT": np.ascontiguousarray(np.asarray(L0_attn_Wq_w, dtype=np.float32).T),
        "wkT": np.ascontiguousarray(np.asarray(L0_attn_Wk_w, dtype=np.float32).T),
        "wvT": np.ascontiguousarray(np.asarray(L0_attn_Wv_w, dtype=np.float32).T),
        "woT": np.ascontiguousarray(np.asarray(L0_attn_Wo_w, dtype=np.float32).T),
        "wfT": np.ascontiguousarray(np.asarray(L0_ffn_weight, dtype=np.float32).T),
    }

    def _btile(b):
        return np.ascontiguousarray(
            np.asarray(b, dtype=np.float32).reshape(8, 128).T)
    bs = {
        "bq": _btile(L0_attn_Wq_b), "bk": _btile(L0_attn_Wk_b),
        "bv": _btile(L0_attn_Wv_b), "bo": _btile(L0_attn_Wo_b),
        "bf": _btile(L0_ffn_bias),
    }
    ident = np.eye(128, dtype=np.float32)
    i_ = np.arange(128)[:, None]
    j_ = np.arange(512)[None, :]
    maskadd = np.concatenate(
        [np.where(o * 128 + i_ >= j_, 0.0, NEG).astype(np.float32)
         for o in range(4)], axis=1).astype(ml_dtypes.bfloat16)

    nc = _get_nc()
    in_maps = []
    for b in range(NCORE):
        im = {"xT": np.ascontiguousarray(x[b].T), "ident": ident, "maskadd": maskadd}
        im.update(ws)
        im.update(bs)
        in_maps.append(im)
    res = run_bass_kernel_spmd(nc, in_maps, core_ids=list(range(NCORE)))
    out = np.stack([np.asarray(res.results[b]["outT"]).T for b in range(NCORE)])
    return np.ascontiguousarray(out).astype(np.float32, copy=False)
